# version 19
# baseline (speedup 1.0000x reference)
import contextlib
import math

import numpy as np
import ml_dtypes

import concourse.bass as bass
import concourse.mybir as mybir
from concourse.bass_utils import run_bass_kernel_spmd

F32 = mybir.dt.float32
BF16 = mybir.dt.bfloat16
FP8 = mybir.dt.float8e4
AF = mybir.ActivationFunctionType
ALU = mybir.AluOpType

LN_EPS = 1e-5
LR = 0.001

FULL_CFG = dict(n_cores=8, NB=2, S=4096, CH=512, D=1024, F=4096, DS=64)


def cfg_derived(cfg):
    c = dict(cfg)
    c["NC_CH"] = c["S"] // c["CH"]
    c["ND"] = c["D"] // 128
    c["NF"] = c["F"] // 128
    c["R"] = c["CH"]
    assert c["R"] <= 512
    c["B_TOT"] = c["n_cores"] * c["NB"]
    c["NTOT"] = c["B_TOT"] * c["CH"] * c["D"]
    return c



def pack_lhsT(w, slab_m):
    K, M = w.shape
    n_kt = K // 128
    n_slabs = M // slab_m
    w3 = np.ascontiguousarray(w.reshape(n_kt, 128, M))
    out = np.empty((n_slabs, 128, n_kt * slab_m), dtype=w.dtype)
    for s in range(n_slabs):
        out[s] = (
            w3[:, :, s * slab_m:(s + 1) * slab_m]
            .transpose(1, 0, 2)
            .reshape(128, n_kt * slab_m)
        )
    return out


def pack_pair(wa, wb, slab_m):
    a = pack_lhsT(wa, slab_m)
    b = pack_lhsT(wb, slab_m)
    return np.concatenate([a, b], axis=2)


def pack_x(x_core, cfg):
    NB, S, D = x_core.shape
    NC_CH, CH, ND, R = cfg["NC_CH"], cfg["CH"], cfg["ND"], cfg["R"]
    v = x_core.reshape(NB, NC_CH, CH, ND, 128)
    v = v.transpose(1, 0, 4, 3, 2)
    return np.ascontiguousarray(v.reshape(NC_CH * NB, 128, ND * R))


def unpack_out(img, cfg):
    NB, NC_CH, CH, ND, R = cfg["NB"], cfg["NC_CH"], cfg["CH"], cfg["ND"], cfg["R"]
    v = img.reshape(NC_CH, NB, 128, ND, R).transpose(1, 0, 4, 3, 2)
    return np.ascontiguousarray(v.reshape(NB, NC_CH * CH, ND * 128))


def pack_percol(v):
    return np.ascontiguousarray(v.reshape(-1, 128).T)



ENGINES = ("tensor", "vector", "scalar", "sync", "gpsimd")

CHANNELS = {
    "sdma": dict(eng="sync", W=6, inc=16),
    "adma": dict(eng="scalar", W=4, inc=16),
    "gdma": dict(eng="gpsimd", W=6, inc=16),
    "mset": dict(eng="vector", W=2, inc=1),
    "cc": dict(eng="gpsimd", W=8, inc=1),
}


class Op:
    __slots__ = ("eng", "fn", "after", "inc", "name", "marked",
                 "chan", "chan_idx", "sem_key", "wait_val", "gate")

    def __init__(self, eng, fn, after, inc, name, chan):
        self.eng = eng
        self.fn = fn
        self.after = after
        self.inc = inc
        self.name = name
        self.chan = chan
        self.chan_idx = None
        self.marked = chan is not None
        self.sem_key = None
        self.wait_val = None
        self.gate = None

    @property
    def is_async(self):
        return self.chan is not None


class Sched:
    def __init__(self):
        self.ops = {e: [] for e in ENGINES}
        self.chan_n = {c: 0 for c in CHANNELS}

    def add(self, eng, fn, after=(), inc=1, name="", chan=None):
        after = [d for d in after if d is not None]
        o = Op(eng, fn, after, inc, name, chan)
        if chan is not None:
            assert CHANNELS[chan]["eng"] == eng, (chan, eng)
            o.chan_idx = self.chan_n[chan]
            self.chan_n[chan] += 1
        self.ops[eng].append(o)
        return o

    def dma(self, eng, fn, after=(), name=""):
        chan = {"sync": "sdma", "gpsimd": "gdma", "scalar": "adma"}[eng]
        return self.add(eng, fn, after, inc=16, name=name, chan=chan)

    def emit(self, nc, engs, sems, csems, same_engine_edges=False,
             final_engine="gpsimd"):
        for e in ENGINES:
            for o in self.ops[e]:
                for d in o.after:
                    d.marked = True
        for e in ENGINES:
            cum = 0
            for o in self.ops[e]:
                if o.is_async:
                    W = CHANNELS[o.chan]["W"]
                    k, lane = divmod(o.chan_idx, W)
                    o.sem_key = (o.chan, lane)
                    o.wait_val = o.inc * (k + 1)
                    if k > 0:
                        o.gate = (o.sem_key, o.inc * k)
                else:
                    if o.marked:
                        cum += o.inc
                    o.sem_key = ("eng", o.eng)
                    o.wait_val = cum
        def semof(key):
            return sems[key[1]] if key[0] == "eng" else csems[key]

        for e in ENGINES:
            eng = engs[e]
            waited = {}
            for o in self.ops[e]:
                need = {}
                if o.gate is not None:
                    need[o.gate[0]] = o.gate[1]
                for d in o.after:
                    if (d.eng == o.eng and not d.is_async
                            and not same_engine_edges):
                        continue
                    k, v = d.sem_key, d.wait_val
                    if v > need.get(k, 0):
                        need[k] = v
                for k in sorted(need, key=repr):
                    v = need[k]
                    if v > waited.get(k, 0):
                        eng.wait_ge(semof(k), v)
                        waited[k] = v
                inst = o.fn(eng)
                if o.marked:
                    inst.then_inc(semof(o.sem_key), o.inc)
        eng = engs[final_engine]
        W = CHANNELS["gdma"]["W"]
        n = self.chan_n["gdma"]
        for lane in range(W):
            uses = (n - lane + W - 1) // W
            if uses > 0:
                eng.wait_ge(csems[("gdma", lane)], 16 * uses)
        for lane in range(CHANNELS["cc"]["W"]):
            uses = (self.chan_n["cc"] - lane + CHANNELS["cc"]["W"] - 1) \
                // CHANNELS["cc"]["W"]
            if uses > 0:
                eng.wait_ge(csems[("cc", lane)], uses)


class Bank:

    def __init__(self, t):
        self.t = t
        self.readers = []

    def take(self):
        r = self.readers
        self.readers = []
        return r


class Slot:

    def __init__(self, t):
        self.t = t
        self.readers = []
        self.load_op = None



def build_program(cfg, same_engine_edges=True):
    cfg = cfg_derived(cfg)
    NB, S, CH, D, F, DS = (cfg["NB"], cfg["S"], cfg["CH"], cfg["D"], cfg["F"],
                           cfg["DS"])
    NC_CH, ND, NF, R, NTOT = (cfg["NC_CH"], cfg["ND"], cfg["NF"], cfg["R"],
                              cfg["NTOT"])
    n_cores = cfg["n_cores"]
    assert DS == 64

    nc = bass.Bass("TRN2", target_bir_lowering=False)

    def din(name, shape, dt=F32):
        return nc.dram_tensor(name, shape, dt, kind="ExternalInput").ap()

    x_hbm = din("x_img", [NC_CH * NB, 128, ND * R])
    out_hbm = nc.dram_tensor("out_img", [NC_CH * NB, 128, ND * R], F32,
                             kind="ExternalOutput").ap()

    SM_A, SM_H = 256, 128
    SM_A8, SM_H8 = 256, 128
    wa02 = din("wa02", [F // SM_A, 128, 2 * ND * SM_A], BF16)
    wa1 = din("wa1", [D // SM_H, 128, NF * SM_H], BF16)
    ww12 = din("ww12", [F // SM_A8, 128, 2 * ND * SM_A8], FP8)
    wh1 = din("wh1", [F // SM_H8, 128, NF * SM_H8], FP8)
    wh2 = din("wh2", [F // SM_H8, 128, NF * SM_H8], FP8)
    ww3 = din("ww3", [D // SM_H8, 128, NF * SM_H8], FP8)

    s1_h = din("s1s", [128, NF])
    s2_h = din("s2s", [128, NF])
    sh1_h = din("sh1s", [128, NF])
    sh2_h = din("sh2s", [128, NF])
    s3_h = din("s3s", [128, ND])
    dvec_h = din("dvec", [128, ND])
    gam_h = din("gam", [128, ND])
    bet_h = din("bet", [128, ND])
    lnr_h = din("lnr", [DS])
    imag_h = din("imag", [DS])
    bt_h = din("btmat", [128, ND * DS])
    ct_h = din("ctmat", [DS, D], BF16)

    cc_in = [nc.dram_tensor(f"ccin{c}", [1, 128], F32) for c in range(NC_CH)]
    cc_kw = {"addr_space": "Shared"} if n_cores > 4 else {}
    cc_out = [nc.dram_tensor(f"ccout{c}", [1, 128], F32, **cc_kw)
              for c in range(NC_CH)]

    sch = Sched()
    st = {}

    with contextlib.ExitStack() as stk:
        def sbuf(name, shape, dt=F32):
            return stk.enter_context(nc.sbuf_tensor(name, shape, dt))

        X = sbuf("X", [128, NB * ND * R])
        lnb = sbuf("lnb", [128, NB * ND * R], BF16)
        NXQ = 4
        xsq = sbuf("xsq", [128, NXQ * R], BF16)
        ufp = sbuf("ufp", [128, ND * R])
        W0 = sbuf("W0", [128, NF * R], BF16)
        W1 = sbuf("W1", [128, NF * R], BF16)
        SLOT_COLS = max(2 * ND * SM_A, NF * SM_H)
        SLOT_COLS8 = max(2 * ND * SM_A8, NF * SM_H8)
        slotA = sbuf("slotA", [128, SLOT_COLS], BF16)
        slotB = sbuf("slotB", [128, SLOT_COLS], BF16)
        slot8A = sbuf("slot8A", [128, SLOT_COLS8], FP8)
        slot8B = sbuf("slot8B", [128, SLOT_COLS8], FP8)
        mu = sbuf("mu", [128, NB * R])
        rstd = sbuf("rstd", [128, NB * R])
        scr0 = sbuf("scr0", [128, R])
        scr1 = sbuf("scr1", [128, R])
        scr2 = sbuf("scr2", [128, R])
        Pr = sbuf("Pr", [64, CH])
        Pi = sbuf("Pi", [64, CH])
        rotr = sbuf("rotr", [64, CH])
        roti = sbuf("roti", [64, CH])
        absa = sbuf("absa", [64, CH])
        busR = sbuf("busR", [64, CH])
        busI = sbuf("busI", [64, CH])
        qR = sbuf("qR", [64, CH])
        qI = sbuf("qI", [64, CH])
        gR = sbuf("gR", [64, CH])
        gI = sbuf("gI", [64, CH])
        hsr = sbuf("hsr", [64, CH])
        hsrb = sbuf("hsrb", [64, CH], BF16)
        carryR = sbuf("carryR", [64, NB])
        carryI = sbuf("carryI", [64, NB])
        bbar = sbuf("bbar", [128, ND * 128], BF16)
        cts = sbuf("cts", [64, D], BF16)
        crep = sbuf("crep", [128, 2 * DS])
        colv = sbuf("colv", [64, 12])
        rowv = sbuf("rowv", [1, 16 * DS])
        bc = sbuf("bc", [64, 6])
        s1s = sbuf("s1s_s", [128, NF])
        s2s = sbuf("s2s_s", [128, NF])
        sh1s = sbuf("sh1s_s", [128, NF])
        sh2s = sbuf("sh2s_s", [128, NF])
        s3s = sbuf("s3s_s", [128, ND])
        dp1 = sbuf("dp1", [128, ND])
        gams = sbuf("gams", [128, ND])
        bets = sbuf("bets", [128, ND])
        ones_bf = sbuf("ones_bf", [128, 128], BF16)
        ones_f1 = sbuf("ones_f1", [1, 128])
        onesc = sbuf("onesc", [128, 1])
        eps_t = sbuf("eps_t", [128, 1])
        negh_t = sbuf("negh_t", [128, 1])
        negone_t = sbuf("negone_t", [128, 1])
        drep = sbuf("drep", [128, 2])
        dcol = sbuf("dcol", [1, 4])
        accs = sbuf("accs", [128, 2 * ND])
        ssec = sbuf("ssec", [128, 2])
        sser = sbuf("sser", [1, 128])
        psum = [stk.enter_context(nc.psum_tensor(f"ps{i}", [128, 512], F32))
                for i in range(8)]
        banks = [Bank(t) for t in psum]
        slots = [Slot(slotA), Slot(slotB)]
        slots8 = [Slot(slot8A), Slot(slot8B)]

        sems = {}
        for e in ENGINES:
            sems[e] = stk.enter_context(nc.semaphore(f"sem_{e}"))
        csems = {}
        for cname, cd in CHANNELS.items():
            for lane in range(cd["W"]):
                csems[(cname, lane)] = stk.enter_context(
                    nc.semaphore(f"c_{cname}{lane}"))

        def Xh(b, dj):
            return X[:, (b * ND + dj) * R:(b * ND + dj + 1) * R]

        def LNh(b, dj):
            return lnb[:, (b * ND + dj) * R:(b * ND + dj + 1) * R]

        slot_i = [0, 0]

        def load_slab(hbm, idx, cols, name, pool=0):
            ss = slots if pool == 0 else slots8
            s = ss[slot_i[pool] % 2]
            slot_i[pool] += 1
            dep = s.readers
            s.readers = []
            op = sch.dma("sync",
                         lambda e, s=s, hbm=hbm, idx=idx, cols=cols:
                         e.dma_start(out=s.t[:, 0:cols], in_=hbm[idx]),
                         after=dep, name=name)
            s.load_op = op
            return s

        def mm_group(bank, pieces, after=(), name="", per_after=None):
            war = bank.take()
            ops = []
            n = len(pieces)
            for i, (lf, rf) in enumerate(pieces):
                a = list(after) + war if i == 0 else list(after)
                if per_after is not None:
                    a = a + [per_after[i]]
                op = sch.add("tensor",
                             lambda e, lf=lf, rf=rf, i=i, n=n, bank=bank:
                             e.matmul(bank.t[0:lf().shape[-1],
                                             0:rf().shape[-1]], lf(), rf(),
                                      start=(i == 0), stop=(i == n - 1)),
                             after=a, name=f"{name}.mm{i}")
                ops.append(op)
            return ops

        setup = []

        def v(fn, after=(), name="", eng="vector", inc=1):
            return sch.add(eng, fn, after=after, inc=inc, name=name)

        def ms(eng, ap_fn, val, after=(), name="memset"):
            return sch.add(eng, lambda e, ap_fn=ap_fn, val=val:
                           e.memset(ap_fn(), val),
                           after=after, name=name, chan="mset")

        ld_ops = {}
        bts = mu
        ld_ops["bts"] = sch.dma("gpsimd",
                                lambda e: e.dma_start(
                                    out=mu[:, 0:ND * DS], in_=bt_h),
                                name="ld_bts")
        for nm, (dst, src) in {
            "s1": (s1s, s1_h), "s2": (s2s, s2_h), "sh1": (sh1s, sh1_h),
            "sh2": (sh2s, sh2_h), "s3": (s3s, s3_h),
            "gam": (gams, gam_h), "bet": (bets, bet_h),
            "cts": (cts, ct_h),
        }.items():
            ld_ops[nm] = sch.dma("gpsimd",
                                 lambda e, d=dst, s=src: e.dma_start(out=d[:], in_=s),
                                 name=f"ld_{nm}")
        op_lnr_c = sch.dma("gpsimd", lambda e: e.dma_start(
            out=colv[:, 0:1], in_=lnr_h.rearrange("(n o) -> n o", o=1)), name="lnr_c")
        op_img_c = sch.dma("gpsimd", lambda e: e.dma_start(
            out=colv[:, 1:2], in_=imag_h.rearrange("(n o) -> n o", o=1)), name="img_c")
        op_lnr_r = sch.dma("gpsimd", lambda e: e.dma_start(
            out=rowv[:, 0:DS], in_=lnr_h.rearrange("(o n) -> o n", o=1)), name="lnr_r")
        op_img_r = sch.dma("gpsimd", lambda e: e.dma_start(
            out=rowv[:, DS:2 * DS], in_=imag_h.rearrange("(o n) -> o n", o=1)),
            name="img_r")
        op_dvec = sch.dma("gpsimd", lambda e: e.dma_start(out=dp1[:], in_=dvec_h),
                          name="ld_dvec")

        m_onesbf = ms("vector", lambda: ones_bf[:], 1.0)
        m_onesf1 = ms("vector", lambda: ones_f1[:], 1.0)
        m_onesc = ms("vector", lambda: onesc[:], 1.0)
        m_eps = ms("vector", lambda: eps_t[:], LN_EPS)
        m_negh = ms("vector", lambda: negh_t[:], -0.5)
        m_nego = ms("vector", lambda: negone_t[:], -1.0)
        m_drep = ms("vector", lambda: drep[:], 1.0)
        m_dcol = ms("vector", lambda: dcol[:], 1.0)
        m_carry = ms("vector", lambda: carryR[:], 0.0)
        m_carry2 = ms("vector", lambda: carryI[:], 0.0)
        m_sser = ms("vector", lambda: sser[:], 0.0)
        op_dp1 = v(lambda e: e.tensor_scalar(dp1[:], dp1[:], 1.0, None, ALU.add),
                   after=[op_dvec], name="dp1")

        TWO_PI = float(2.0 * math.pi)
        HALF_PI = float(0.5 * math.pi)
        PI = float(math.pi)

        MAGIC = float(1.5 * 2 ** 23)
        INV2PI = float(1.0 / TWO_PI)
        _c1 = float(np.float32(6.28125))
        _r1 = TWO_PI - _c1
        _c2 = float(np.float32(int(_r1 * 2 ** 21) / 2 ** 21))
        _c3 = float(TWO_PI - _c1 - _c2)

        def range_reduce(dst_sl, src_sl, add0, after, tag, scr_sl):
            o1 = v(lambda e: e.tensor_scalar(dst_sl(), src_sl(), add0, INV2PI,
                                             ALU.add, ALU.mult),
                   after=after, name=f"{tag}.q")
            o2 = v(lambda e: e.tensor_scalar(scr_sl(), dst_sl(), MAGIC, MAGIC,
                                             ALU.add, ALU.subtract),
                   after=[o1], name=f"{tag}.k")
            o3 = v(lambda e: e.tensor_scalar(dst_sl(), src_sl(), add0, None,
                                             ALU.add),
                   after=[o2], name=f"{tag}.y")
            o4 = v(lambda e: e.scalar_tensor_tensor(dst_sl(), scr_sl(), -_c1,
                                                    dst_sl(), ALU.mult,
                                                    ALU.add),
                   after=[o2, o3], name=f"{tag}.c1")
            o5 = v(lambda e: e.scalar_tensor_tensor(dst_sl(), scr_sl(), -_c2,
                                                    dst_sl(), ALU.mult,
                                                    ALU.add),
                   after=[o4], name=f"{tag}.c2")
            o6 = v(lambda e: e.scalar_tensor_tensor(dst_sl(), scr_sl(), -_c3,
                                                    dst_sl(), ALU.mult,
                                                    ALU.add),
                   after=[o5], name=f"{tag}.c3")
            return o6

        c_e1 = v(lambda e: e.activation(colv[:, 2:3], colv[:, 0:1], AF.Exp),
                 after=[op_lnr_c], name="c_e1", eng="scalar")
        c_abs = v(lambda e: e.activation(colv[:, 3:4], colv[:, 2:3], AF.Exp,
                                         scale=negone_t[0:64, :]),
                  after=[c_e1, m_nego], name="c_abs", eng="scalar")
        c_th = range_reduce(lambda: colv[:, 4:5], lambda: colv[:, 1:2], 0.0,
                            [op_img_c], "c_th", lambda: colv[:, 8:9])
        c_th2 = range_reduce(lambda: colv[:, 5:6], lambda: colv[:, 1:2], HALF_PI,
                             [op_img_c, c_th], "c_th2", lambda: colv[:, 8:9])
        c_sin = v(lambda e: e.activation(colv[:, 6:7], colv[:, 4:5], AF.Sin),
                  after=[c_th], name="c_sin", eng="scalar")
        c_cos = v(lambda e: e.activation(colv[:, 7:8], colv[:, 5:6], AF.Sin),
                  after=[c_th2], name="c_cos", eng="scalar")
        c_p1r = v(lambda e: e.tensor_copy(bc[:, 4:5], colv[:, 7:8]),
                  after=[c_cos], name="p1r")
        c_p1i = v(lambda e: e.tensor_scalar(bc[:, 5:6], colv[:, 6:7], -1.0, None,
                                            ALU.mult),
                  after=[c_sin], name="p1i")
        m_p0r = ms("vector", lambda: Pr[:, 0:1], 1.0)
        m_p0i = ms("vector", lambda: Pi[:, 0:1], 0.0)
        lastP = [m_p0r, m_p0i, c_p1r, c_p1i]
        seg = 1
        curr, curi = c_p1r, c_p1i
        while seg < CH:
            w = min(seg, CH - seg)
            a1_ = v(lambda e, seg=seg, w=w: e.tensor_scalar(
                rotr[:, 0:w], Pr[:, 0:w], bc[:, 4:5], None, ALU.mult),
                after=lastP, name=f"P{seg}.a")
            a2_ = v(lambda e, seg=seg, w=w: e.tensor_scalar(
                roti[:, 0:w], Pi[:, 0:w], bc[:, 5:6], None, ALU.mult),
                after=lastP, name=f"P{seg}.b")
            a3_ = v(lambda e, seg=seg, w=w: e.tensor_scalar(
                scr0[0:64, 0:w], Pi[:, 0:w], bc[:, 4:5], None, ALU.mult),
                after=lastP, name=f"P{seg}.c")
            a4_ = v(lambda e, seg=seg, w=w: e.tensor_scalar(
                scr1[0:64, 0:w], Pr[:, 0:w], bc[:, 5:6], None, ALU.mult),
                after=lastP, name=f"P{seg}.d")
            a5_ = v(lambda e, seg=seg, w=w: e.tensor_tensor(
                Pr[:, seg:seg + w], rotr[:, 0:w], roti[:, 0:w], ALU.subtract),
                after=[a1_, a2_], name=f"P{seg}.e")
            a6_ = v(lambda e, seg=seg, w=w: e.tensor_tensor(
                Pi[:, seg:seg + w], scr0[0:64, 0:w], scr1[0:64, 0:w], ALU.add),
                after=[a3_, a4_], name=f"P{seg}.f")
            b1_ = v(lambda e: e.tensor_tensor(bc[:, 2:3], bc[:, 4:5], bc[:, 4:5],
                                              ALU.mult),
                    after=[a1_, a2_, a3_, a4_, curr, curi], name=f"P{seg}.g")
            b2_ = v(lambda e: e.tensor_tensor(bc[:, 3:4], bc[:, 5:6], bc[:, 5:6],
                                              ALU.mult),
                    after=[a1_, a2_, a3_, a4_, curr, curi], name=f"P{seg}.h")
            b3_ = v(lambda e: e.tensor_tensor(bc[:, 5:6], bc[:, 4:5], bc[:, 5:6],
                                              ALU.mult),
                    after=[b1_, b2_], name=f"P{seg}.i")
            b4_ = v(lambda e: e.tensor_scalar(bc[:, 5:6], bc[:, 5:6], 2.0, None,
                                              ALU.mult),
                    after=[b3_], name=f"P{seg}.j")
            b5_ = v(lambda e: e.tensor_tensor(bc[:, 4:5], bc[:, 2:3], bc[:, 3:4],
                                              ALU.subtract),
                    after=[b3_], name=f"P{seg}.k")
            curr, curi = b5_, b4_
            lastP = [a5_, a6_, b5_, b4_]
            seg *= 2
        p5a = v(lambda e: e.tensor_tensor(bc[:, 2:3], Pr[:, CH - 1:CH],
                                          colv[:, 7:8], ALU.mult),
                after=lastP + [c_cos], name="p512.a")
        p5b = v(lambda e: e.tensor_scalar(scr0[0:64, 0:1], Pi[:, CH - 1:CH],
                                          colv[:, 6:7], None, ALU.mult),
                after=lastP + [c_sin], name="p512.b")
        p5c = v(lambda e: e.tensor_tensor(bc[:, 2:3], bc[:, 2:3],
                                          scr0[0:64, 0:1], ALU.add),
                after=[p5a, p5b], name="p512.c")
        p5d = v(lambda e: e.tensor_scalar(scr0[0:64, 0:1], Pr[:, CH - 1:CH],
                                          colv[:, 6:7], None, ALU.mult),
                after=[p5c], name="p512.d")
        p5e = v(lambda e: e.tensor_scalar(scr1[0:64, 0:1], Pi[:, CH - 1:CH],
                                          colv[:, 7:8], None, ALU.mult),
                after=[p5c], name="p512.e")
        p5g = v(lambda e: e.tensor_tensor(bc[:, 3:4], scr1[0:64, 0:1],
                                          scr0[0:64, 0:1], ALU.subtract),
                after=[p5d, p5e], name="p512.g")
        op_p512 = [p5c, p5g]
        m_bcr = ms("vector", lambda: bc[:, 0:1], 1.0)
        m_bci = ms("vector", lambda: bc[:, 1:2], 0.0)
        m_absa = ms("vector", lambda: absa[:], 1.0)
        ab1 = v(lambda e: e.tensor_scalar(absa[:], absa[:],
                                          colv[:, 3:4], None, ALU.mult),
                after=[m_absa, c_abs], name="absa.s")
        op_absa = ab1

        def rv(i):
            return rowv[:, i * DS:(i + 1) * DS]

        r_e1 = v(lambda e: e.activation(rv(2), rv(0), AF.Exp),
                 after=[op_lnr_r], name="r_e1", eng="scalar")
        r_th = range_reduce(lambda: rv(4), lambda: rv(1), 0.0, [op_img_r],
                            "r_th", lambda: rv(13))
        r_th2 = range_reduce(lambda: rv(5), lambda: rv(1), HALF_PI,
                             [op_img_r, r_th], "r_th2", lambda: rv(13))
        r_sin = v(lambda e: e.activation(rv(6), rv(4), AF.Sin),
                  after=[r_th], name="r_sin", eng="scalar")
        r_cos = v(lambda e: e.activation(rv(7), rv(5), AF.Sin),
                  after=[r_th2], name="r_cos", eng="scalar")
        r_expn = v(lambda e: e.activation(rv(8), rv(2), AF.Exp,
                                          scale=negone_t[0:1, :]),
                   after=[r_e1, m_nego], name="r_expn", eng="scalar")
        r_n1 = v(lambda e: e.tensor_tensor(rv(9), rv(8), rv(7), ALU.mult),
                 after=[r_expn, r_cos], name="r_n1")
        r_n2 = v(lambda e: e.tensor_scalar(rv(9), rv(9), -1.0, None, ALU.add),
                 after=[r_n1], name="r_n2")
        r_n3 = v(lambda e: e.tensor_tensor(rv(13), rv(8), rv(6), ALU.mult),
                 after=[r_expn, r_sin], name="r_n3")
        r_d1 = v(lambda e: e.tensor_tensor(rv(10), rv(2), rv(2), ALU.mult),
                 after=[r_e1], name="r_d1")
        r_d2 = v(lambda e: e.tensor_tensor(rv(14), rv(1), rv(1), ALU.mult),
                 after=[op_img_r], name="r_d2")
        r_d3 = v(lambda e: e.tensor_tensor(rv(10), rv(10), rv(14), ALU.add),
                 after=[r_d1, r_d2], name="r_d3")
        r_d4 = v(lambda e: e.reciprocal(rv(10), rv(10)), after=[r_d3],
                 name="r_d4")
        r_c1 = v(lambda e: e.tensor_tensor(rv(11), rv(9), rv(2), ALU.mult),
                 after=[r_n2, r_e1], name="r_c1")
        r_c2 = v(lambda e: e.tensor_scalar(rv(11), rv(11), -1.0, None, ALU.mult),
                 after=[r_c1], name="r_c2")
        r_c3 = v(lambda e: e.tensor_tensor(rv(14), rv(13), rv(1), ALU.mult),
                 after=[r_n3], name="r_c3")
        r_c4 = v(lambda e: e.tensor_tensor(rv(11), rv(11), rv(14), ALU.add),
                 after=[r_c2, r_c3], name="r_c4")
        r_c5 = v(lambda e: e.tensor_tensor(rv(11), rv(11), rv(10), ALU.mult),
                 after=[r_c4, r_d4], name="r_c5")
        r_c6 = v(lambda e: e.tensor_tensor(rv(12), rv(13), rv(2), ALU.mult),
                 after=[r_n3, r_e1], name="r_c6")
        r_c7 = v(lambda e: e.tensor_scalar(rv(12), rv(12), -1.0, None, ALU.mult),
                 after=[r_c6], name="r_c7")
        r_c8 = v(lambda e: e.tensor_tensor(rv(14), rv(9), rv(1), ALU.mult),
                 after=[r_n2], name="r_c8")
        r_c9 = v(lambda e: e.tensor_tensor(rv(12), rv(12), rv(14), ALU.subtract),
                 after=[r_c7, r_c8], name="r_c9")
        r_c10 = v(lambda e: e.tensor_tensor(rv(12), rv(12), rv(10), ALU.mult),
                  after=[r_c9, r_d4], name="r_c10")
        bco = mm_group(banks[6], [
            (lambda: ones_f1[:], lambda: rowv[:, 11 * DS:13 * DS]),
        ], after=[r_c5, r_c10, m_onesf1], name="bcoef")
        bcc = v(lambda e: e.copy(crep[:], banks[6].t[:, 0:2 * DS]),
                after=[bco[-1]], name="bcoef.c", eng="scalar")
        banks[6].readers.append(bcc)
        bb_ops = []
        for dj in range(ND):
            o1 = v(lambda e, dj=dj: e.tensor_tensor(
                bbar[:, dj * 128:dj * 128 + DS],
                bts[:, dj * DS:(dj + 1) * DS], crep[:, 0:DS], ALU.mult),
                after=[bcc, ld_ops["bts"]], name=f"bbar{dj}.r")
            o2 = v(lambda e, dj=dj: e.tensor_tensor(
                bbar[:, dj * 128 + DS:(dj + 1) * 128],
                bts[:, dj * DS:(dj + 1) * DS], crep[:, DS:2 * DS], ALU.mult),
                after=[bcc, ld_ops["bts"]], name=f"bbar{dj}.i")
            bb_ops += [o1, o2]

        Xw = {}
        lnw = {}
        ln_readers = {}
        x_load = {}
        store_ops = []
        drep_w = [m_drep]
        drep_r = []
        carry_w = {b: [m_carry, m_carry2] for b in range(NB)}
        rot_w = []
        rot_r = []
        bc_w = [m_bcr, m_bci] + op_p512
        ufp_w = {}
        ufp_r = []
        acc_w = {}
        ssec_r = []
        dcol_r = []
        bus_r = []
        qrot_r = []
        gsc_r = []
        hsr_r = []
        hsrb_r = []
        W0w = {}
        W1w = {}
        W0r = []
        W1r = []
        xsq_w = {}
        xsq_r = {k: [] for k in range(NXQ)}
        mu_w = {0: list(bb_ops), 1: list(bb_ops)}
        scrw = {0: [p5c, p5g], 1: [p5g], 2: []}

        def emit_stats(tag, b, xb_ops):
            mmo1 = mm_group(banks[6], [
                (lambda: ones_bf[:], lambda b=b, dj=dj: LNh(b, dj))
                for dj in range(ND)],
                after=xb_ops + [m_onesbf], name=f"{tag}.st1")
            war7 = banks[7].take()
            sq_ops = []
            mmo2 = []
            for dj in range(ND):
                sq = v(lambda e, dj=dj, b=b: e.tensor_tensor(
                    xsq[:, (dj % NXQ) * R:(dj % NXQ + 1) * R], LNh(b, dj),
                    LNh(b, dj), ALU.mult),
                    after=[xb_ops[dj]] + xsq_r[dj % NXQ],
                    name=f"{tag}.sq{dj}")
                sq_ops.append(sq)
                a = [sq, m_onesbf] + (war7 if dj == 0 else [])
                piece = sch.add(
                    "tensor",
                    lambda e, dj=dj: e.matmul(
                        banks[7].t[:, 0:R], ones_bf[:],
                        xsq[:, (dj % NXQ) * R:(dj % NXQ + 1) * R],
                        start=(dj == 0), stop=(dj == ND - 1)),
                    after=a, name=f"{tag}.st2.{dj}")
                mmo2.append(piece)
                xsq_r[dj % NXQ] = [piece]
            mu_sl = mu[:, b * R:(b + 1) * R]
            rs_sl = rstd[:, b * R:(b + 1) * R]
            o_mu = v(lambda e, mu_sl=mu_sl: e.tensor_scalar(
                mu_sl, banks[6].t[:, 0:R], 1.0 / D, None, ALU.mult),
                after=[mmo1[-1]] + mu_w.get(b, []), name=f"{tag}.mu")
            banks[6].readers.append(o_mu)
            o_msq = v(lambda e, mu_sl=mu_sl: e.tensor_tensor(
                scr2[:], mu_sl, mu_sl, ALU.mult),
                after=[o_mu] + scrw[2], name=f"{tag}.msq")
            o_var = v(lambda e, rs_sl=rs_sl: e.scalar_tensor_tensor(
                rs_sl, banks[7].t[:, 0:R], 1.0 / D, scr2[:],
                ALU.mult, ALU.subtract),
                after=[mmo2[-1], o_msq], name=f"{tag}.var")
            banks[7].readers.append(o_var)
            scrw[2] = [o_msq, o_var]
            o_lnv = v(lambda e, rs_sl=rs_sl: e.activation(
                scr2[:], rs_sl, AF.Ln, bias=eps_t[:]),
                after=[o_var, m_eps] + scrw[2], name=f"{tag}.lnv",
                eng="scalar")
            o_rstd = v(lambda e, rs_sl=rs_sl: e.activation(
                rs_sl, scr2[:], AF.Exp, scale=negh_t[:]),
                after=[o_lnv, m_negh], name=f"{tag}.rstd", eng="scalar")
            scrw[2] = [o_lnv, o_rstd]
            mu_w[b] = [o_rstd]
            return mmo1, sq_ops, o_mu, o_rstd, mu_sl, rs_sl

        def emit_LN(c, b, want_ufp, tag, x_dep_extra=()):
            xb_ops = []
            war = ln_readers.get(b, [])
            for dj in range(ND):
                dep = [Xw.get((b, dj))] + list(x_dep_extra) + war
                if dj % 2 == 0:
                    o = v(lambda e, b=b, dj=dj: e.copy(LNh(b, dj), Xh(b, dj)),
                          after=dep, name=f"{tag}.xb{dj}", eng="scalar")
                else:
                    o = v(lambda e, b=b, dj=dj: e.tensor_copy(LNh(b, dj),
                                                              Xh(b, dj)),
                          after=dep, name=f"{tag}.xb{dj}", eng="gpsimd")
                xb_ops.append(o)
            ln_readers[b] = []
            mmo1, sq_ops, o_mu, o_rstd, mu_sl, rs_sl = emit_stats(
                tag, b, xb_ops)
            norm_ops = []
            for dj in range(ND):
                lane = dj % 2
                eng = "vector" if lane == 0 else "gpsimd"
                sc = scr0 if lane == 0 else scr1
                o_s = v(lambda e, b=b, dj=dj, mu_sl=mu_sl, sc=sc:
                        e.tensor_tensor(sc[:], Xh(b, dj), mu_sl,
                                        ALU.subtract),
                        after=[Xw.get((b, dj)), o_mu] + scrw[lane],
                        name=f"{tag}.n{dj}a", eng=eng)
                scrw[lane] = [o_s]
                if want_ufp:
                    o_u = v(lambda e, dj=dj, rs_sl=rs_sl, sc=sc:
                            e.tensor_tensor(
                                ufp[:, dj * R:(dj + 1) * R], sc[:], rs_sl,
                                ALU.mult),
                            after=[o_s, o_rstd] + ufp_r, name=f"{tag}.n{dj}u",
                            eng=eng)
                    o_n = v(lambda e, b=b, dj=dj: e.copy(
                        LNh(b, dj), ufp[:, dj * R:(dj + 1) * R]),
                        after=[o_u, xb_ops[dj]] + mmo1 + [sq_ops[dj]],
                        name=f"{tag}.n{dj}c", eng="scalar")
                    ufp_w[dj] = o_u
                    scrw[lane].append(o_u)
                    norm_ops.append(o_n)
                else:
                    o_n = v(lambda e, b=b, dj=dj, rs_sl=rs_sl, sc=sc:
                            e.tensor_tensor(LNh(b, dj), sc[:], rs_sl,
                                            ALU.mult),
                            after=[o_s, o_rstd, xb_ops[dj]] + mmo1 +
                            [sq_ops[dj]],
                            name=f"{tag}.n{dj}", eng=eng)
                    scrw[lane].append(o_n)
                    norm_ops.append(o_n)
            if want_ufp:
                ufp_r.clear()
            return norm_ops

        NRH = R

        Wbuf = [W0, W1]
        scrh = [scr0, scr1]

        def Wr(b):
            return W0r if b == 0 else W1r

        def load_x(c, b):
            dep = [st.get(("store", c - 1, b))] if c > 0 else []
            o = sch.dma("gpsimd",
                        lambda e, c=c, b=b: e.dma_start(
                            out=X[:, b * ND * R:(b + 1) * ND * R],
                            in_=x_hbm[c * NB + b]),
                        after=dep, name=f"x{c}.{b}")
            for dj in range(ND):
                Xw[(b, dj)] = o

        def store_x(c, b):
            dep = [Xw[(b, dj)] for dj in range(ND)]
            o_st = sch.dma("gpsimd",
                           lambda e, c=c, b=b: e.dma_start(
                               out=out_hbm[c * NB + b],
                               in_=X[:, b * ND * R:(b + 1) * ND * R]),
                           after=dep, name=f"store{c}.{b}")
            st[("store", c, b)] = o_st

        def s1A(c, b, norm):
            h_evac = {}
            for s in range(F // SM_A):
                slab = load_slab(wa02, s, 2 * ND * SM_A, f"a02.{c}.{b}.{s}")
                for fl in range(SM_A // 128):
                    ft = s * (SM_A // 128) + fl
                    bk0 = banks[(ft % 2) * 2]
                    bk1 = banks[(ft % 2) * 2 + 1]
                    mm0 = mm_group(bk0, [
                        (lambda slab=slab, kt=kt, fl=fl:
                         slab.t[:, kt * SM_A + fl * 128:
                                kt * SM_A + (fl + 1) * 128],
                         lambda b=b, kt=kt: LNh(b, kt))
                        for kt in range(ND)],
                        after=[slab.load_op], name=f"p0.{ft}.{b}",
                        per_after=norm)
                    mm1 = mm_group(bk1, [
                        (lambda slab=slab, kt=kt, fl=fl:
                         slab.t[:, ND * SM_A + kt * SM_A + fl * 128:
                                ND * SM_A + kt * SM_A + (fl + 1) * 128],
                         lambda b=b, kt=kt: LNh(b, kt))
                        for kt in range(ND)],
                        after=[slab.load_op], name=f"p2.{ft}.{b}",
                        per_after=norm)
                    slab.readers += [mm0[-1], mm1[-1]]
                    tscr = scrh[b]
                    o_t1 = v(lambda e, bk0=bk0, tscr=tscr: e.activation(
                        tscr[:], bk0.t[:, 0:NRH], AF.Silu,
                        scale=drep[:, 0:1]),
                        after=[mm0[-1]] + drep_w + scrw[b],
                        name=f"s1h.{ft}.{b}.a", eng="scalar")
                    drep_r.append(o_t1)
                    bk0.readers.append(o_t1)
                    o_h = v(lambda e, bk1=bk1, ft=ft, b=b, tscr=tscr:
                            e.tensor_tensor(
                                Wbuf[b][:, ft * NRH:(ft + 1) * NRH],
                                tscr[:], bk1.t[:, 0:NRH], ALU.mult),
                            after=[mm1[-1], o_t1] + Wr(b),
                            name=f"s1h.{ft}.{b}.b")
                    bk1.readers.append(o_h)
                    scrw[b] = [o_h]
                    h_evac[(b, ft)] = o_h
            if b == 0:
                W0r.clear()
            else:
                W1r.clear()
            return h_evac

        def s1B(c, b, norm, h_evac):
            loss_ops = []
            for s in range(D // SM_H):
                slab = load_slab(wa1, s, NF * SM_H, f"a1.{c}.{b}.{s}")
                for dl in range(SM_H // 128):
                    dj = s * (SM_H // 128) + dl
                    bk = banks[4 + (dj % 2)]
                    mmo = mm_group(bk, [
                        (lambda slab=slab, kt=kt, dl=dl:
                         slab.t[:, kt * SM_H + dl * 128:
                                kt * SM_H + (dl + 1) * 128],
                         lambda b=b, kt=kt:
                         Wbuf[b][:, kt * NRH:(kt + 1) * NRH])
                        for kt in range(NF)],
                        after=[slab.load_op],
                        per_after=[h_evac[(b, kt)] for kt in range(NF)],
                        name=f"aout.{dj}.{b}")
                    slab.readers.append(mmo[-1])
                    Wr(b).append(mmo[-1])
                    sA = scrh[b]
                    o_o = v(lambda e, bk=bk, sA=sA: e.tensor_scalar(
                        sA[:], bk.t[:, 0:NRH], drep[:, 1:2], None, ALU.mult),
                        after=[mmo[-1]] + drep_w + scrw[b],
                        name=f"al.{dj}.{b}.o")
                    drep_r.append(o_o)
                    bk.readers.append(o_o)
                    o_x = v(lambda e, b=b, dj=dj, sA=sA: e.tensor_tensor(
                        Xh(b, dj), Xh(b, dj), sA[:], ALU.add),
                        after=[o_o, Xw[(b, dj)]], name=f"al.{dj}.{b}.x")
                    Xw[(b, dj)] = o_x
                    o_d = v(lambda e, b=b, dj=dj, sA=sA: e.tensor_tensor(
                        sA[:], sA[:], LNh(b, dj), ALU.subtract),
                        after=[o_o, norm[dj]], name=f"al.{dj}.{b}.d")
                    ln_readers.setdefault(b, []).append(o_d)
                    o_q = v(lambda e, b=b, dj=dj, sA=sA:
                            e.scalar_tensor_tensor(
                                sA[:], sA[:], 1.0, sA[:], ALU.mult, ALU.mult,
                                accum_out=accs[:, b * ND + dj:
                                               b * ND + dj + 1]),
                            after=[o_d] + acc_w.get((b, dj), []),
                            name=f"al.{dj}.{b}.q")
                    scrw[b] = [o_d, o_q]
                    acc_w[(b, dj)] = [o_q]
                    loss_ops.append(o_q)
            st[("loss", c, b)] = loss_ops

        def sse_cc(c):
            lo = []
            for b in range(NB):
                lo += st[("loss", c, b)]
            o_r1 = v(lambda e: e.reduce_sum(ssec[:, 0:1], accs[:],
                                            axis=mybir.AxisListType.X),
                     after=lo + ssec_r, name=f"sse{c}.r")
            ssec_r.clear()
            for k in list(acc_w):
                acc_w[k] = [o_r1]
            mm_s = mm_group(banks[6], [
                (lambda: onesc[:], lambda: ssec[:, 0:1])],
                after=[o_r1, m_onesc], name=f"sse{c}.mm")
            ssec_r.append(mm_s[-1])
            o_sc = v(lambda e: e.copy(sser[:, 0:1], banks[6].t[0:1, 0:1]),
                     after=[mm_s[-1], m_sser] + st.get(("sserow_r", c), []),
                     name=f"sse{c}.c", eng="scalar")
            banks[6].readers.append(o_sc)
            o_wr = sch.dma("gpsimd",
                           lambda e, c=c: e.dma_start(out=cc_in[c].ap(),
                                                      in_=sser[:]),
                           after=[o_sc], name=f"sse{c}.wr")
            st[("sserow_r", c + 1)] = [o_wr]

            def cc_fn(e, c=c):
                return e.collective_compute(
                    "AllReduce", ALU.add,
                    replica_groups=[list(range(n_cores))],
                    ins=[cc_in[c].ap().opt()],
                    outs=[cc_out[c].ap().opt()])
            st[("cc", c)] = sch.add("gpsimd", cc_fn, after=[o_wr], inc=1,
                                    name=f"cc{c}", chan="cc")

        def d_update(c):
            o_cc = st[("cc", c)]
            o_rd = sch.dma("gpsimd",
                           lambda e, c=c: e.dma_start(out=sser[:],
                                                      in_=cc_out[c].ap()),
                           after=[o_cc, st[("sserow_r", c + 1)][0]],
                           name=f"ccrd{c}")
            o_dc = v(lambda e: e.tensor_scalar(
                dcol[:, 1:2], sser[0:1, 0:1], -LR / NTOT, 1.0,
                ALU.mult, ALU.add),
                after=[o_rd, m_dcol] + dcol_r, name=f"dec{c}")
            st[("sserow_r", c + 1)] = [o_dc]
            o_dm = v(lambda e: e.tensor_tensor(
                dcol[:, 0:1], dcol[:, 0:1], dcol[:, 1:2], ALU.mult),
                after=[o_dc], name=f"dmul{c}")
            o_d2 = v(lambda e: e.tensor_tensor(
                dcol[:, 2:3], dcol[:, 0:1], dcol[:, 0:1], ALU.mult),
                after=[o_dm], name=f"d2{c}")
            mm_d = mm_group(banks[7], [
                (lambda: ones_f1[:], lambda: dcol[:, 0:3])],
                after=[o_d2, m_onesf1], name=f"dbc{c}")
            dcol_r.clear()
            dcol_r.append(mm_d[-1])
            o_dr = v(lambda e: e.copy(drep[:], banks[7].t[:, 0:3:2]),
                     after=[mm_d[-1]] + drep_r, name=f"drep{c}", eng="scalar")
            banks[7].readers.append(o_dr)
            drep_w.clear()
            drep_w.append(o_dr)
            drep_r.clear()

        def ffn_w12(tagp, b, norm, preslabs=()):
            g_evac = {}
            for s in range(F // SM_A8):
                if s < len(preslabs):
                    slab = preslabs[s]
                else:
                    slab = load_slab(ww12, s, 2 * ND * SM_A8,
                                     f"{tagp}.w12.{b}.{s}", pool=1)
                for fl in range(SM_A8 // 128):
                    ft = s * (SM_A8 // 128) + fl
                    bk0 = banks[(ft % 2) * 2]
                    bk1 = banks[(ft % 2) * 2 + 1]
                    mm0 = mm_group(bk0, [
                        (lambda slab=slab, kt=kt, fl=fl:
                         slab.t[:, kt * SM_A8 + fl * 128:
                                kt * SM_A8 + (fl + 1) * 128],
                         lambda b=b, kt=kt: LNh(b, kt))
                        for kt in range(ND)],
                        after=[slab.load_op], name=f"{tagp}.w1.{ft}",
                        per_after=norm)
                    mm1 = mm_group(bk1, [
                        (lambda slab=slab, kt=kt, fl=fl:
                         slab.t[:, ND * SM_A8 + kt * SM_A8 + fl * 128:
                                ND * SM_A8 + kt * SM_A8 + (fl + 1) * 128],
                         lambda b=b, kt=kt: LNh(b, kt))
                        for kt in range(ND)],
                        after=[slab.load_op], name=f"{tagp}.w2.{ft}",
                        per_after=norm)
                    slab.readers += [mm0[-1], mm1[-1]]
                    tscr = scr0 if ft % 2 == 0 else scr1
                    o_t1 = v(lambda e, bk0=bk0, ft=ft, tscr=tscr:
                             e.activation(
                                 tscr[:], bk0.t[:, 0:NRH], AF.Silu,
                                 scale=s1s[:, ft:ft + 1]),
                             after=[mm0[-1], ld_ops["s1"]] + scrw[ft % 2],
                             name=f"{tagp}.g{ft}.a", eng="scalar")
                    bk0.readers.append(o_t1)
                    o_g = v(lambda e, bk1=bk1, ft=ft, tscr=tscr:
                            e.scalar_tensor_tensor(
                                W0[:, ft * NRH:(ft + 1) * NRH],
                                bk1.t[:, 0:NRH], s2s[:, ft:ft + 1],
                                tscr[:], ALU.mult, ALU.mult),
                            after=[mm1[-1], o_t1, ld_ops["s2"]] + W0r,
                            name=f"{tagp}.g{ft}.b")
                    bk1.readers.append(o_g)
                    scrw[ft % 2] = [o_g]
                    g_evac[ft] = o_g
            W0r.clear()
            return g_evac

        def ffn_h(tagp, phs, whbm, b, src_evac, srcW, dstW, dstWr, sc_sl,
                  bank_base, cb=None):
            out_evac = {}
            for s in range(F // SM_H8):
                slab = load_slab(whbm, s, NF * SM_H8,
                                 f"{tagp}.{phs}.{b}.{s}", pool=1)
                for ml in range(SM_H8 // 128):
                    mt = s * (SM_H8 // 128) + ml
                    if cb is not None and mt in cb:
                        cb.pop(mt)()
                    bk = banks[bank_base + (mt % 2)]
                    mmo = mm_group(bk, [
                        (lambda slab=slab, kt=kt, ml=ml:
                         slab.t[:, kt * SM_H8 + ml * 128:
                                kt * SM_H8 + (ml + 1) * 128],
                         lambda kt=kt: srcW[:, kt * NRH:(kt + 1) * NRH])
                        for kt in range(NF)],
                        after=[slab.load_op],
                        per_after=[src_evac[kt] for kt in range(NF)],
                        name=f"{tagp}.{phs}.{mt}")
                    slab.readers.append(mmo[-1])
                    (W0r if srcW is W0 else W1r).append(mmo[-1])
                    o_gx = v(lambda e, bk=bk, mt=mt: e.activation(
                        dstW[:, mt * NRH:(mt + 1) * NRH],
                        bk.t[:, 0:NRH], AF.Gelu, scale=sc_sl[:, mt:mt + 1]),
                        after=[mmo[-1], ld_ops["sh1" if phs == "h1"
                                               else "sh2"]] + dstWr,
                        name=f"{tagp}.g{phs}.{mt}", eng="scalar")
                    bk.readers.append(o_gx)
                    out_evac[mt] = o_gx
            dstWr.clear()
            return out_evac

        def ffn_w3(tagp, b, g2_evac):
            for s in range(D // SM_H8):
                slab = load_slab(ww3, s, NF * SM_H8,
                                 f"{tagp}.w3.{b}.{s}", pool=1)
                for dl in range(SM_H8 // 128):
                    dj = s * (SM_H8 // 128) + dl
                    bk = banks[2 + (dj % 2)]
                    mmo = mm_group(bk, [
                        (lambda slab=slab, kt=kt, dl=dl:
                         slab.t[:, kt * SM_H8 + dl * 128:
                                kt * SM_H8 + (dl + 1) * 128],
                         lambda kt=kt: W0[:, kt * NRH:(kt + 1) * NRH])
                        for kt in range(NF)],
                        after=[slab.load_op],
                        per_after=[g2_evac[kt] for kt in range(NF)],
                        name=f"{tagp}.w3.{dj}")
                    slab.readers.append(mmo[-1])
                    W0r.append(mmo[-1])
                    o_x = v(lambda e, bk=bk, b=b, dj=dj:
                            e.scalar_tensor_tensor(
                                Xh(b, dj), bk.t[:, 0:NRH],
                                s3s[:, dj:dj + 1], Xh(b, dj),
                                ALU.mult, ALU.add),
                            after=[mmo[-1], Xw[(b, dj)], ld_ops["s3"]],
                            name=f"{tagp}.xo.{dj}")
                    bk.readers.append(o_x)
                    Xw[(b, dj)] = o_x

        def s3_rot(c):
            nonlocal rot_w, bc_w
            rr1 = v(lambda e: e.tensor_scalar(rotr[:], Pr[:], bc[:, 0:1],
                                              None, ALU.mult),
                    after=lastP + bc_w + rot_r, name=f"rot{c}.a")
            rr2 = v(lambda e: e.tensor_scalar(scr0[0:64, 0:CH], Pi[:],
                                              bc[:, 1:2], None, ALU.mult),
                    after=lastP + bc_w + scrw[0], name=f"rot{c}.b")
            rr3 = v(lambda e: e.tensor_tensor(rotr[:], rotr[:],
                                              scr0[0:64, 0:CH],
                                              ALU.subtract),
                    after=[rr1, rr2], name=f"rot{c}.c")
            ri1 = v(lambda e: e.tensor_scalar(roti[:], Pi[:], bc[:, 0:1],
                                              None, ALU.mult),
                    after=lastP + bc_w + rot_r, name=f"rot{c}.d")
            ri2 = v(lambda e: e.tensor_scalar(scr0[0:64, 0:CH], Pr[:],
                                              bc[:, 1:2], None, ALU.mult),
                    after=[rr3] + bc_w, name=f"rot{c}.e")
            ri3 = v(lambda e: e.tensor_tensor(roti[:], roti[:],
                                              scr0[0:64, 0:CH], ALU.add),
                    after=[ri1, ri2], name=f"rot{c}.f")
            scrw[0] = [ri2, ri3]
            rot_w = [rr3, ri3]
            rot_r.clear()
            u1 = v(lambda e: e.tensor_tensor(colv[:, 9:10], bc[:, 0:1],
                                             bc[:, 2:3], ALU.mult),
                   after=[rr3, ri1] + bc_w, name=f"bc{c}.a")
            u2 = v(lambda e: e.tensor_tensor(colv[:, 10:11], bc[:, 1:2],
                                             bc[:, 3:4], ALU.mult),
                   after=[rr3, ri1] + bc_w, name=f"bc{c}.b")
            u3 = v(lambda e: e.tensor_tensor(colv[:, 11:12], bc[:, 0:1],
                                             bc[:, 3:4], ALU.mult),
                   after=[rr3, ri1] + bc_w, name=f"bc{c}.c")
            u4 = v(lambda e: e.tensor_tensor(bc[:, 1:2], bc[:, 1:2],
                                             bc[:, 2:3], ALU.mult),
                   after=[u1, u2, u3] + bc_w + [ri3, ri2], name=f"bc{c}.d")
            u5 = v(lambda e: e.tensor_tensor(bc[:, 0:1], colv[:, 9:10],
                                             colv[:, 10:11], ALU.subtract),
                   after=[u1, u2, u4], name=f"bc{c}.e")
            u6 = v(lambda e: e.tensor_tensor(bc[:, 1:2], colv[:, 11:12],
                                             bc[:, 1:2], ALU.add),
                   after=[u3, u4], name=f"bc{c}.f")
            bc_w = [u5, u6]

        def s3_scan(c, b, norm):
            mmb = mm_group(banks[4], [
                (lambda dj=dj: bbar[:, dj * 128:(dj + 1) * 128],
                 lambda b=b, dj=dj: LNh(b, dj))
                for dj in range(ND)],
                after=list(bb_ops), per_after=norm, name=f"bu{c}.{b}")
            ln_readers.setdefault(b, []).extend(mmb)
            o_bu = v(lambda e: e.copy(busR[:], banks[4].t[0:64, 0:CH]),
                     after=[mmb[-1]] + bus_r, name=f"bu{c}.{b}.r",
                     eng="scalar")
            o_bu2 = v(lambda e: e.copy(busI[:], banks[4].t[64:128, 0:CH]),
                      after=[mmb[-1]] + bus_r, name=f"bu{c}.{b}.i",
                      eng="scalar")
            bus_r.clear()
            banks[4].readers.extend([o_bu, o_bu2])
            q1 = v(lambda e: e.tensor_tensor(scr0[0:64, 0:CH], busR[:],
                                             rotr[:], ALU.mult),
                   after=[o_bu] + rot_w + scrw[0], name=f"q{c}{b}.1")
            q2 = v(lambda e: e.tensor_tensor(scr1[0:64, 0:CH], busI[:],
                                             roti[:], ALU.mult),
                   after=[o_bu2] + rot_w + scrw[1], name=f"q{c}{b}.2")
            q3 = v(lambda e: e.tensor_tensor(qR[:], scr0[0:64, 0:CH],
                                             scr1[0:64, 0:CH], ALU.subtract),
                   after=[q1, q2] + qrot_r, name=f"q{c}{b}.3")
            q4 = v(lambda e: e.tensor_tensor(scr0[0:64, 0:CH], busR[:],
                                             roti[:], ALU.mult),
                   after=[q3, o_bu] + rot_w, name=f"q{c}{b}.4")
            q5 = v(lambda e: e.tensor_tensor(scr1[0:64, 0:CH], busI[:],
                                             rotr[:], ALU.mult),
                   after=[q3, o_bu2] + rot_w, name=f"q{c}{b}.5")
            q6 = v(lambda e: e.tensor_tensor(qI[:], scr0[0:64, 0:CH],
                                             scr1[0:64, 0:CH], ALU.add),
                   after=[q4, q5] + qrot_r, name=f"q{c}{b}.6")
            qrot_r.clear()
            scrw[0] = [q4, q6]
            scrw[1] = [q5, q6]
            rot_r.extend([q1, q2, q4, q5])
            sc1 = v(lambda e, b=b: e.tensor_tensor_scan(
                gR[:], absa[:], qR[:], carryR[:, b:b + 1], ALU.mult, ALU.add),
                after=[q3, op_absa] + carry_w[b] + gsc_r,
                name=f"scan{c}{b}.r")
            sc2 = v(lambda e, b=b: e.tensor_tensor_scan(
                gI[:], absa[:], qI[:], carryI[:, b:b + 1], ALU.mult, ALU.add),
                after=[q6, op_absa] + carry_w[b] + gsc_r,
                name=f"scan{c}{b}.i")
            gsc_r.clear()
            qrot_r.extend([sc1, sc2])
            o_cw = v(lambda e, b=b: e.tensor_copy(
                carryR[:, b:b + 1], gR[:, CH - 1:CH]),
                after=[sc1, sc2], name=f"carry{c}{b}.r")
            o_cw2 = v(lambda e, b=b: e.tensor_copy(
                carryI[:, b:b + 1], gI[:, CH - 1:CH]),
                after=[sc1, sc2], name=f"carry{c}{b}.i")
            carry_w[b] = [o_cw, o_cw2]
            gsc_r.extend([o_cw, o_cw2])
            h1_ = v(lambda e: e.tensor_tensor(scr0[0:64, 0:CH], gR[:],
                                              rotr[:], ALU.mult),
                    after=[sc1] + rot_w + scrw[0], name=f"un{c}{b}.1")
            h2_ = v(lambda e: e.tensor_tensor(scr1[0:64, 0:CH], gI[:],
                                              roti[:], ALU.mult),
                    after=[sc2] + rot_w + scrw[1], name=f"un{c}{b}.2")
            h3_ = v(lambda e: e.tensor_tensor(hsr[:], scr0[0:64, 0:CH],
                                              scr1[0:64, 0:CH], ALU.add),
                    after=[h1_, h2_] + hsr_r, name=f"un{c}{b}.3")
            hsr_r.clear()
            gsc_r.extend([h1_, h2_])
            scrw[0] = [h1_, h3_]
            scrw[1] = [h2_, h3_]
            rot_r.extend([h1_, h2_])
            o_hb = v(lambda e: e.copy(hsrb[:], hsr[:]),
                     after=[h3_] + hsrb_r, name=f"un{c}{b}.c", eng="scalar")
            hsrb_r.clear()
            hsr_r.append(o_hb)
            bus_r.extend([q1, q2, q4, q5])
            return o_hb

        def s3_y(c, b, o_hb):
            t_ops = []
            for dj in range(ND):
                bk = banks[5 + (dj % 2)]
                mmo = mm_group(bk, [
                    (lambda dj=dj: cts[0:64, dj * 128:(dj + 1) * 128],
                     lambda: hsrb[:]),
                ], after=[o_hb, ld_ops["cts"]], name=f"y{c}{b}.{dj}")
                hsrb_r.append(mmo[-1])
                o_t = v(lambda e, bk=bk, dj=dj: e.scalar_tensor_tensor(
                    ufp[:, dj * R:(dj + 1) * R],
                    ufp[:, dj * R:(dj + 1) * R], dp1[:, dj:dj + 1],
                    bk.t[:, 0:NRH], ALU.mult, ALU.add),
                    after=[mmo[-1], ufp_w[dj], op_dp1], name=f"t{c}{b}.{dj}")
                bk.readers.append(o_t)
                ufp_w[dj] = o_t
                t_ops.append(o_t)
            return t_ops

        def s3_ln2(c, b, t_ops):
            war = ln_readers.get(b, [])
            xb2 = []
            for dj in range(ND):
                if dj % 2 == 0:
                    o = v(lambda e, b=b, dj=dj: e.copy(
                        LNh(b, dj), ufp[:, dj * R:(dj + 1) * R]),
                        after=[t_ops[dj]] + war, name=f"ln2x{dj}",
                        eng="scalar")
                else:
                    o = v(lambda e, b=b, dj=dj: e.tensor_copy(
                        LNh(b, dj), ufp[:, dj * R:(dj + 1) * R]),
                        after=[t_ops[dj]] + war, name=f"ln2x{dj}",
                        eng="gpsimd")
                xb2.append(o)
            ln_readers[b] = []
            mmo1, sq2, o_mu, o_rstd, mu_sl, rs_sl = emit_stats(
                f"c{c}b{b}.ln2", b, xb2)
            ln_readers[b] = list(mmo1) + list(sq2)
            for dj in range(ND):
                lane = dj % 2
                eng = "vector" if lane == 0 else "gpsimd"
                sc = scrh[lane]
                o_s = v(lambda e, dj=dj, mu_sl=mu_sl, sc=sc: e.tensor_tensor(
                    sc[:], ufp[:, dj * R:(dj + 1) * R], mu_sl, ALU.subtract),
                    after=[t_ops[dj], o_mu] + scrw[lane],
                    name=f"ln2n{dj}a", eng=eng)
                ufp_r.append(o_s)
                o_m = v(lambda e, rs_sl=rs_sl, sc=sc: e.tensor_tensor(
                    sc[:], sc[:], rs_sl, ALU.mult),
                    after=[o_s, o_rstd], name=f"ln2n{dj}b", eng=eng)
                o_g = v(lambda e, b=b, dj=dj, sc=sc: e.scalar_tensor_tensor(
                    sc[:], sc[:], gams[:, dj:dj + 1], Xh(b, dj),
                    ALU.mult, ALU.add),
                    after=[o_m, Xw[(b, dj)], ld_ops["gam"]],
                    name=f"ln2n{dj}c")
                o_xn = v(lambda e, b=b, dj=dj, sc=sc: e.tensor_scalar(
                    Xh(b, dj), sc[:], bets[:, dj:dj + 1], None, ALU.add),
                    after=[o_g, ld_ops["bet"]], name=f"ln2n{dj}d")
                scrw[lane] = [o_g, o_xn]
                Xw[(b, dj)] = o_xn

        for b in range(NB):
            load_x(0, b)
        norm1 = {b: emit_LN(0, b, False, f"c0b{b}.ln1") for b in range(NB)}
        for c in range(NC_CH):
            pre12 = [load_slab(ww12, s, 2 * ND * SM_A8,
                               f"c{c}s2.w12p.{s}", pool=1)
                     for s in range(min(2, F // SM_A8))]
            h0 = s1A(c, 0, norm1[0])
            if c > 0:
                norm1[1] = emit_LN(c, 1, False, f"c{c}b1.ln1")
            s1B(c, 0, norm1[0], h0)
            h1v = s1A(c, 1, norm1[1])
            norm2_0 = emit_LN(c, 0, False, f"c{c}s2b0.ln")
            s1B(c, 1, norm1[1], h1v)
            g = ffn_w12(f"c{c}s2", 0, norm2_0, pre12)
            if c < NC_CH - 1:
                sse_cc(c)
            norm2_1 = emit_LN(c, 1, False, f"c{c}s2b1.ln")
            g1 = ffn_h(f"c{c}s2", "h1", wh1, 0, g, W0, W1, W1r, sh1s, 4)
            g2 = ffn_h(f"c{c}s2", "h2", wh2, 0, g1, W1, W0, W0r, sh2s, 0)
            ffn_w3(f"c{c}s2", 0, g2)
            g = ffn_w12(f"c{c}s2", 1, norm2_1)
            norm3_0 = emit_LN(c, 0, True, f"c{c}b0.ln3")
            box = {}
            K1, K2 = max(1, NF // 4), max(2, 3 * NF // 4)
            cb = {K1: lambda: (s3_rot(c),
                               box.__setitem__('hb', s3_scan(c, 0, norm3_0))),
                  K2: lambda: box.__setitem__('t', s3_y(c, 0, box['hb']))}
            g1 = ffn_h(f"c{c}s2", "h1", wh1, 1, g, W0, W1, W1r, sh1s, 4, cb)
            cb = {K1: lambda: s3_ln2(c, 0, box['t']),
                  K2: lambda: box.__setitem__(
                      'n4', emit_LN(c, 0, False, f"c{c}s4b0.ln"))}
            g2 = ffn_h(f"c{c}s2", "h2", wh2, 1, g1, W1, W0, W0r, sh2s, 0, cb)
            norm4_0 = box['n4']
            ffn_w3(f"c{c}s2", 1, g2)
            g = ffn_w12(f"c{c}s4", 0, norm4_0)
            norm3_1 = emit_LN(c, 1, True, f"c{c}b1.ln3")
            cb = {K1: lambda: box.__setitem__('hb', s3_scan(c, 1, norm3_1)),
                  K2: lambda: box.__setitem__('t', s3_y(c, 1, box['hb']))}
            g1 = ffn_h(f"c{c}s4", "h1", wh1, 0, g, W0, W1, W1r, sh1s, 4, cb)
            cb = {K1: lambda: s3_ln2(c, 1, box['t']),
                  K2: lambda: box.__setitem__(
                      'n41', emit_LN(c, 1, False, f"c{c}s4b1.ln"))}
            g2 = ffn_h(f"c{c}s4", "h2", wh2, 0, g1, W1, W0, W0r, sh2s, 0, cb)
            norm4_1 = box['n41']
            ffn_w3(f"c{c}s4", 0, g2)
            store_x(c, 0)
            g = ffn_w12(f"c{c}s4", 1, norm4_1)
            if c < NC_CH - 1:
                load_x(c + 1, 0)
                d_update(c)
            cb = {}
            if c < NC_CH - 1:
                cb = {K2: lambda: norm1.__setitem__(
                    0, emit_LN(c + 1, 0, False, f"c{c + 1}b0.ln1"))}
            g1 = ffn_h(f"c{c}s4", "h1", wh1, 1, g, W0, W1, W1r, sh1s, 4, cb)
            g2 = ffn_h(f"c{c}s4", "h2", wh2, 1, g1, W1, W0, W0r, sh2s, 0)
            ffn_w3(f"c{c}s4", 1, g2)
            store_x(c, 1)
            if c < NC_CH - 1:
                load_x(c + 1, 1)

        sch.emit(nc, {"tensor": nc.tensor, "vector": nc.vector,
                      "scalar": nc.scalar, "sync": nc.sync,
                      "gpsimd": nc.gpsimd},
                 sems, csems, same_engine_edges=same_engine_edges)

    return nc



def pack_core_inputs(cfg, core, x_full, wd):
    cfg = cfg_derived(cfg)
    NB = cfg["NB"]
    xi = pack_x(np.ascontiguousarray(x_full[core * NB:(core + 1) * NB]), cfg)
    im = {"x_img": xi}
    im.update(wd)
    return im


def make_weight_images(cfg, a0, a1, a2, w1, w2, h1, h2, w3,
                       s1, s2, sh1, sh2, s3, lnr, imag, Bm, Cm, Dv, gam, bet):
    cfg = cfg_derived(cfg)
    bf = ml_dtypes.bfloat16
    f8 = mybir.dt.np(FP8)
    wd = {
        "wa02": pack_pair(a0.astype(bf), a2.astype(bf), 256),
        "wa1": pack_lhsT(a1.astype(bf), 128),
        "ww12": pack_pair(np.ascontiguousarray(w1.T).astype(f8),
                          np.ascontiguousarray(w2.T).astype(f8), 256),
        "wh1": pack_lhsT(np.ascontiguousarray(h1.T).astype(f8), 128),
        "wh2": pack_lhsT(np.ascontiguousarray(h2.T).astype(f8), 128),
        "ww3": pack_lhsT(np.ascontiguousarray(w3.T).astype(f8), 128),
        "s1s": pack_percol(s1.astype(np.float32)),
        "s2s": pack_percol(s2.astype(np.float32)),
        "sh1s": pack_percol(sh1.astype(np.float32)),
        "sh2s": pack_percol(sh2.astype(np.float32)),
        "s3s": pack_percol(s3.astype(np.float32)),
        "dvec": pack_percol(Dv.astype(np.float32)),
        "gam": pack_percol(gam.astype(np.float32)),
        "bet": pack_percol(bet.astype(np.float32)),
        "lnr": lnr.astype(np.float32),
        "imag": imag.astype(np.float32),
        "btmat": pack_btmat(Bm, cfg),
        "ctmat": np.ascontiguousarray(Cm.T).astype(bf),
    }
    return wd


def pack_btmat(Bm, cfg):
    DS, D = Bm.shape
    ND = D // 128
    bt = np.ascontiguousarray(Bm.T)
    v = bt.reshape(ND, 128, DS).transpose(1, 0, 2).reshape(128, ND * DS)
    return np.ascontiguousarray(v.astype(np.float32))


def kernel(L0_attn_a_x, L0_attn_a_w0, L0_attn_a_w1, L0_attn_a_w2,
           L0_attn_b_log_neg_real, L0_attn_b_imag, L0_attn_b_B_mat,
           L0_attn_b_C_mat, L0_attn_b_D, L0_attn_b_gamma, L0_attn_b_beta,
           L0_ffn_w1_w, L0_ffn_w1_s, L0_ffn_w2_w, L0_ffn_w2_s,
           L0_ffn_h1_w, L0_ffn_h1_s, L0_ffn_h2_w, L0_ffn_h2_s,
           L0_ffn_w3_w, L0_ffn_w3_s):
    cfg = cfg_derived(FULL_CFG)
    x = np.asarray(L0_attn_a_x, dtype=np.float32)
    wd = make_weight_images(
        cfg,
        np.asarray(L0_attn_a_w0, np.float32), np.asarray(L0_attn_a_w1, np.float32),
        np.asarray(L0_attn_a_w2, np.float32),
        np.asarray(L0_ffn_w1_w, np.float32), np.asarray(L0_ffn_w2_w, np.float32),
        np.asarray(L0_ffn_h1_w, np.float32), np.asarray(L0_ffn_h2_w, np.float32),
        np.asarray(L0_ffn_w3_w, np.float32),
        np.asarray(L0_ffn_w1_s, np.float32), np.asarray(L0_ffn_w2_s, np.float32),
        np.asarray(L0_ffn_h1_s, np.float32), np.asarray(L0_ffn_h2_s, np.float32),
        np.asarray(L0_ffn_w3_s, np.float32),
        np.asarray(L0_attn_b_log_neg_real, np.float32),
        np.asarray(L0_attn_b_imag, np.float32),
        np.asarray(L0_attn_b_B_mat, np.float32),
        np.asarray(L0_attn_b_C_mat, np.float32),
        np.asarray(L0_attn_b_D, np.float32),
        np.asarray(L0_attn_b_gamma, np.float32),
        np.asarray(L0_attn_b_beta, np.float32),
    )
    nc = build_program(FULL_CFG)
    in_maps = [pack_core_inputs(cfg, core, x, wd)
               for core in range(cfg["n_cores"])]
    res = run_bass_kernel_spmd(nc, in_maps, core_ids=list(range(cfg["n_cores"])))
    outs = []
    for core in range(cfg["n_cores"]):
        outs.append(unpack_out(res.results[core]["out_img"], cfg))
    return np.concatenate(outs, axis=0)


# version 20
# speedup vs baseline: 1.0491x; 1.0491x over previous
import contextlib
import math

import numpy as np
import ml_dtypes

import concourse.bass as bass
import concourse.mybir as mybir
from concourse.bass_utils import run_bass_kernel_spmd

F32 = mybir.dt.float32
BF16 = mybir.dt.bfloat16
FP8 = mybir.dt.float8e4
AF = mybir.ActivationFunctionType
ALU = mybir.AluOpType

LN_EPS = 1e-5
LR = 0.001

FULL_CFG = dict(n_cores=8, NB=2, S=4096, CH=512, D=1024, F=4096, DS=64)


def cfg_derived(cfg):
    c = dict(cfg)
    c["NC_CH"] = c["S"] // c["CH"]
    c["ND"] = c["D"] // 128
    c["NF"] = c["F"] // 128
    c["R"] = c["CH"]
    assert c["R"] <= 512
    c["B_TOT"] = c["n_cores"] * c["NB"]
    c["NTOT"] = c["B_TOT"] * c["CH"] * c["D"]
    return c



def pack_lhsT(w, slab_m):
    K, M = w.shape
    n_kt = K // 128
    n_slabs = M // slab_m
    w3 = np.ascontiguousarray(w.reshape(n_kt, 128, M))
    out = np.empty((n_slabs, 128, n_kt * slab_m), dtype=w.dtype)
    for s in range(n_slabs):
        out[s] = (
            w3[:, :, s * slab_m:(s + 1) * slab_m]
            .transpose(1, 0, 2)
            .reshape(128, n_kt * slab_m)
        )
    return out


def pack_pair(wa, wb, slab_m):
    a = pack_lhsT(wa, slab_m)
    b = pack_lhsT(wb, slab_m)
    return np.concatenate([a, b], axis=2)


def pack_x(x_core, cfg):
    NB, S, D = x_core.shape
    NC_CH, CH, ND, R = cfg["NC_CH"], cfg["CH"], cfg["ND"], cfg["R"]
    v = x_core.reshape(NB, NC_CH, CH, ND, 128)
    v = v.transpose(1, 0, 4, 3, 2)
    return np.ascontiguousarray(v.reshape(NC_CH * NB, 128, ND * R))


def unpack_out(img, cfg):
    NB, NC_CH, CH, ND, R = cfg["NB"], cfg["NC_CH"], cfg["CH"], cfg["ND"], cfg["R"]
    v = img.reshape(NC_CH, NB, 128, ND, R).transpose(1, 0, 4, 3, 2)
    return np.ascontiguousarray(v.reshape(NB, NC_CH * CH, ND * 128))


def pack_percol(v):
    return np.ascontiguousarray(v.reshape(-1, 128).T)



ENGINES = ("tensor", "vector", "scalar", "sync", "gpsimd")

CHANNELS = {
    "sdma": dict(eng="sync", W=6, inc=16),
    "adma": dict(eng="scalar", W=4, inc=16),
    "gdma": dict(eng="gpsimd", W=6, inc=16),
    "mset": dict(eng="vector", W=2, inc=1),
    "cc": dict(eng="gpsimd", W=8, inc=1),
}


class Op:
    __slots__ = ("eng", "fn", "after", "inc", "name", "marked",
                 "chan", "chan_idx", "sem_key", "wait_val", "gate")

    def __init__(self, eng, fn, after, inc, name, chan):
        self.eng = eng
        self.fn = fn
        self.after = after
        self.inc = inc
        self.name = name
        self.chan = chan
        self.chan_idx = None
        self.marked = chan is not None
        self.sem_key = None
        self.wait_val = None
        self.gate = None

    @property
    def is_async(self):
        return self.chan is not None


class Sched:
    def __init__(self):
        self.ops = {e: [] for e in ENGINES}
        self.chan_n = {c: 0 for c in CHANNELS}

    def add(self, eng, fn, after=(), inc=1, name="", chan=None):
        after = [d for d in after if d is not None]
        o = Op(eng, fn, after, inc, name, chan)
        if chan is not None:
            assert CHANNELS[chan]["eng"] == eng, (chan, eng)
            o.chan_idx = self.chan_n[chan]
            self.chan_n[chan] += 1
        self.ops[eng].append(o)
        return o

    def dma(self, eng, fn, after=(), name=""):
        chan = {"sync": "sdma", "gpsimd": "gdma", "scalar": "adma"}[eng]
        return self.add(eng, fn, after, inc=16, name=name, chan=chan)

    def emit(self, nc, engs, sems, csems, same_engine_edges=False,
             final_engine="gpsimd"):
        for e in ENGINES:
            for o in self.ops[e]:
                for d in o.after:
                    d.marked = True
        for e in ENGINES:
            cum = 0
            for o in self.ops[e]:
                if o.is_async:
                    W = CHANNELS[o.chan]["W"]
                    k, lane = divmod(o.chan_idx, W)
                    o.sem_key = (o.chan, lane)
                    o.wait_val = o.inc * (k + 1)
                    if k > 0:
                        o.gate = (o.sem_key, o.inc * k)
                else:
                    if o.marked:
                        cum += o.inc
                    o.sem_key = ("eng", o.eng)
                    o.wait_val = cum
        def semof(key):
            return sems[key[1]] if key[0] == "eng" else csems[key]

        for e in ENGINES:
            eng = engs[e]
            waited = {}
            for o in self.ops[e]:
                need = {}
                if o.gate is not None:
                    need[o.gate[0]] = o.gate[1]
                for d in o.after:
                    if (d.eng == o.eng and not d.is_async
                            and not same_engine_edges):
                        continue
                    k, v = d.sem_key, d.wait_val
                    if v > need.get(k, 0):
                        need[k] = v
                for k in sorted(need, key=repr):
                    v = need[k]
                    if v > waited.get(k, 0):
                        eng.wait_ge(semof(k), v)
                        waited[k] = v
                inst = o.fn(eng)
                if o.marked:
                    inst.then_inc(semof(o.sem_key), o.inc)
        eng = engs[final_engine]
        W = CHANNELS["gdma"]["W"]
        n = self.chan_n["gdma"]
        for lane in range(W):
            uses = (n - lane + W - 1) // W
            if uses > 0:
                eng.wait_ge(csems[("gdma", lane)], 16 * uses)
        for lane in range(CHANNELS["cc"]["W"]):
            uses = (self.chan_n["cc"] - lane + CHANNELS["cc"]["W"] - 1) \
                // CHANNELS["cc"]["W"]
            if uses > 0:
                eng.wait_ge(csems[("cc", lane)], uses)


class Bank:

    def __init__(self, t):
        self.t = t
        self.readers = []

    def take(self):
        r = self.readers
        self.readers = []
        return r


class Slot:

    def __init__(self, t):
        self.t = t
        self.readers = []
        self.load_op = None



def build_program(cfg, same_engine_edges=True):
    cfg = cfg_derived(cfg)
    NB, S, CH, D, F, DS = (cfg["NB"], cfg["S"], cfg["CH"], cfg["D"], cfg["F"],
                           cfg["DS"])
    NC_CH, ND, NF, R, NTOT = (cfg["NC_CH"], cfg["ND"], cfg["NF"], cfg["R"],
                              cfg["NTOT"])
    n_cores = cfg["n_cores"]
    assert DS == 64

    nc = bass.Bass("TRN2", target_bir_lowering=False)

    def din(name, shape, dt=F32):
        return nc.dram_tensor(name, shape, dt, kind="ExternalInput").ap()

    x_hbm = din("x_img", [NC_CH * NB, 128, ND * R])
    out_hbm = nc.dram_tensor("out_img", [NC_CH * NB, 128, ND * R], F32,
                             kind="ExternalOutput").ap()

    SM_A, SM_H = 256, 128
    SM_A8, SM_H8 = 256, 128
    wa02 = din("wa02", [F // SM_A, 128, 2 * ND * SM_A], BF16)
    wa1 = din("wa1", [D // SM_H, 128, NF * SM_H], BF16)
    ww12 = din("ww12", [F // SM_A8, 128, 2 * ND * SM_A8], FP8)
    wh1 = din("wh1", [F // SM_H8, 128, NF * SM_H8], FP8)
    wh2 = din("wh2", [F // SM_H8, 128, NF * SM_H8], FP8)
    ww3 = din("ww3", [D // SM_H8, 128, NF * SM_H8], FP8)

    s1_h = din("s1s", [128, NF])
    s2_h = din("s2s", [128, NF])
    sh1_h = din("sh1s", [128, NF])
    sh2_h = din("sh2s", [128, NF])
    s3_h = din("s3s", [128, ND])
    dvec_h = din("dvec", [128, ND])
    gam_h = din("gam", [128, ND])
    bet_h = din("bet", [128, ND])
    lnr_h = din("lnr", [DS])
    imag_h = din("imag", [DS])
    bt_h = din("btmat", [128, ND * DS])
    ct_h = din("ctmat", [DS, D], BF16)

    cc_in = [nc.dram_tensor(f"ccin{c}", [1, 128], F32) for c in range(NC_CH)]
    cc_kw = {"addr_space": "Shared"} if n_cores > 4 else {}
    cc_out = [nc.dram_tensor(f"ccout{c}", [1, 128], F32, **cc_kw)
              for c in range(NC_CH)]

    sch = Sched()
    st = {}

    with contextlib.ExitStack() as stk:
        def sbuf(name, shape, dt=F32):
            return stk.enter_context(nc.sbuf_tensor(name, shape, dt))

        X = sbuf("X", [128, NB * ND * R])
        lnb = sbuf("lnb", [128, NB * ND * R], BF16)
        NXQ = 4
        xsq = sbuf("xsq", [128, NXQ * R], BF16)
        ufp = sbuf("ufp", [128, ND * R])
        W0 = sbuf("W0", [128, NF * R], BF16)
        W1 = sbuf("W1", [128, NF * R], BF16)
        SLOT_COLS = max(2 * ND * SM_A, NF * SM_H)
        SLOT_COLS8 = max(2 * ND * SM_A8, NF * SM_H8)
        slotA = sbuf("slotA", [128, SLOT_COLS], BF16)
        slotB = sbuf("slotB", [128, SLOT_COLS], BF16)
        slot8A = sbuf("slot8A", [128, SLOT_COLS8], FP8)
        slot8B = sbuf("slot8B", [128, SLOT_COLS8], FP8)
        mu = sbuf("mu", [128, NB * R])
        rstd = sbuf("rstd", [128, NB * R])
        scr0 = sbuf("scr0", [128, R])
        scr1 = sbuf("scr1", [128, R])
        scr2 = sbuf("scr2", [128, R])
        Pr = sbuf("Pr", [64, CH])
        Pi = sbuf("Pi", [64, CH])
        rotr = sbuf("rotr", [64, CH])
        roti = sbuf("roti", [64, CH])
        absa = sbuf("absa", [64, CH])
        busR = sbuf("busR", [64, CH])
        busI = sbuf("busI", [64, CH])
        qR = sbuf("qR", [64, CH])
        qI = sbuf("qI", [64, CH])
        gR = sbuf("gR", [64, CH])
        gI = sbuf("gI", [64, CH])
        hsr = sbuf("hsr", [64, CH])
        hsrb = sbuf("hsrb", [64, CH], BF16)
        carryR = sbuf("carryR", [64, NB])
        carryI = sbuf("carryI", [64, NB])
        bbar = sbuf("bbar", [128, ND * 128], BF16)
        cts = sbuf("cts", [64, D], BF16)
        crep = sbuf("crep", [128, 2 * DS])
        colv = sbuf("colv", [64, 12])
        rowv = sbuf("rowv", [1, 16 * DS])
        bc = sbuf("bc", [64, 6])
        s1s = sbuf("s1s_s", [128, NF])
        s2s = sbuf("s2s_s", [128, NF])
        sh1s = sbuf("sh1s_s", [128, NF])
        sh2s = sbuf("sh2s_s", [128, NF])
        s3s = sbuf("s3s_s", [128, ND])
        dp1 = sbuf("dp1", [128, ND])
        gams = sbuf("gams", [128, ND])
        bets = sbuf("bets", [128, ND])
        ones_bf = sbuf("ones_bf", [128, 128], BF16)
        ones_f1 = sbuf("ones_f1", [1, 128])
        onesc = sbuf("onesc", [128, 1])
        eps_t = sbuf("eps_t", [128, 1])
        negh_t = sbuf("negh_t", [128, 1])
        negone_t = sbuf("negone_t", [128, 1])
        drep = sbuf("drep", [128, 2])
        dcol = sbuf("dcol", [1, 4])
        accs = sbuf("accs", [128, 2 * ND])
        ssec = sbuf("ssec", [128, 2])
        sser = sbuf("sser", [1, 128])
        psum = [stk.enter_context(nc.psum_tensor(f"ps{i}", [128, 512], F32))
                for i in range(8)]
        banks = [Bank(t) for t in psum]
        slots = [Slot(slotA), Slot(slotB)]
        slots8 = [Slot(slot8A), Slot(slot8B)]

        sems = {}
        for e in ENGINES:
            sems[e] = stk.enter_context(nc.semaphore(f"sem_{e}"))
        csems = {}
        for cname, cd in CHANNELS.items():
            for lane in range(cd["W"]):
                csems[(cname, lane)] = stk.enter_context(
                    nc.semaphore(f"c_{cname}{lane}"))

        def Xh(b, dj):
            return X[:, (b * ND + dj) * R:(b * ND + dj + 1) * R]

        def LNh(b, dj):
            return lnb[:, (b * ND + dj) * R:(b * ND + dj + 1) * R]

        slot_i = [0, 0]

        def load_slab(hbm, idx, cols, name, pool=0):
            ss = slots if pool == 0 else slots8
            s = ss[slot_i[pool] % 2]
            slot_i[pool] += 1
            dep = s.readers
            s.readers = []
            op = sch.dma("sync",
                         lambda e, s=s, hbm=hbm, idx=idx, cols=cols:
                         e.dma_start(out=s.t[:, 0:cols], in_=hbm[idx]),
                         after=dep, name=name)
            s.load_op = op
            return s

        def mm_group(bank, pieces, after=(), name="", per_after=None):
            war = bank.take()
            ops = []
            n = len(pieces)
            for i, (lf, rf) in enumerate(pieces):
                a = list(after) + war if i == 0 else list(after)
                if per_after is not None:
                    a = a + [per_after[i]]
                op = sch.add("tensor",
                             lambda e, lf=lf, rf=rf, i=i, n=n, bank=bank:
                             e.matmul(bank.t[0:lf().shape[-1],
                                             0:rf().shape[-1]], lf(), rf(),
                                      start=(i == 0), stop=(i == n - 1)),
                             after=a, name=f"{name}.mm{i}")
                ops.append(op)
            return ops

        setup = []

        def v(fn, after=(), name="", eng="vector", inc=1):
            return sch.add(eng, fn, after=after, inc=inc, name=name)

        def ms(eng, ap_fn, val, after=(), name="memset"):
            return sch.add(eng, lambda e, ap_fn=ap_fn, val=val:
                           e.memset(ap_fn(), val),
                           after=after, name=name, chan="mset")

        ld_ops = {}
        bts = mu
        ld_ops["bts"] = sch.dma("gpsimd",
                                lambda e: e.dma_start(
                                    out=mu[:, 0:ND * DS], in_=bt_h),
                                name="ld_bts")
        for nm, (dst, src) in {
            "s1": (s1s, s1_h), "s2": (s2s, s2_h), "sh1": (sh1s, sh1_h),
            "sh2": (sh2s, sh2_h), "s3": (s3s, s3_h),
            "gam": (gams, gam_h), "bet": (bets, bet_h),
            "cts": (cts, ct_h),
        }.items():
            ld_ops[nm] = sch.dma("gpsimd",
                                 lambda e, d=dst, s=src: e.dma_start(out=d[:], in_=s),
                                 name=f"ld_{nm}")
        op_lnr_c = sch.dma("gpsimd", lambda e: e.dma_start(
            out=colv[:, 0:1], in_=lnr_h.rearrange("(n o) -> n o", o=1)), name="lnr_c")
        op_img_c = sch.dma("gpsimd", lambda e: e.dma_start(
            out=colv[:, 1:2], in_=imag_h.rearrange("(n o) -> n o", o=1)), name="img_c")
        op_lnr_r = sch.dma("gpsimd", lambda e: e.dma_start(
            out=rowv[:, 0:DS], in_=lnr_h.rearrange("(o n) -> o n", o=1)), name="lnr_r")
        op_img_r = sch.dma("gpsimd", lambda e: e.dma_start(
            out=rowv[:, DS:2 * DS], in_=imag_h.rearrange("(o n) -> o n", o=1)),
            name="img_r")
        op_dvec = sch.dma("gpsimd", lambda e: e.dma_start(out=dp1[:], in_=dvec_h),
                          name="ld_dvec")

        m_onesbf = ms("vector", lambda: ones_bf[:], 1.0)
        m_onesf1 = ms("vector", lambda: ones_f1[:], 1.0)
        m_onesc = ms("vector", lambda: onesc[:], 1.0)
        m_eps = ms("vector", lambda: eps_t[:], LN_EPS)
        m_negh = ms("vector", lambda: negh_t[:], -0.5)
        m_nego = ms("vector", lambda: negone_t[:], -1.0)
        m_drep = ms("vector", lambda: drep[:], 1.0)
        m_dcol = ms("vector", lambda: dcol[:], 1.0)
        m_carry = ms("vector", lambda: carryR[:], 0.0)
        m_carry2 = ms("vector", lambda: carryI[:], 0.0)
        m_sser = ms("vector", lambda: sser[:], 0.0)
        op_dp1 = v(lambda e: e.tensor_scalar(dp1[:], dp1[:], 1.0, None, ALU.add),
                   after=[op_dvec], name="dp1")

        TWO_PI = float(2.0 * math.pi)
        HALF_PI = float(0.5 * math.pi)
        PI = float(math.pi)

        MAGIC = float(1.5 * 2 ** 23)
        INV2PI = float(1.0 / TWO_PI)
        _c1 = float(np.float32(6.28125))
        _r1 = TWO_PI - _c1
        _c2 = float(np.float32(int(_r1 * 2 ** 21) / 2 ** 21))
        _c3 = float(TWO_PI - _c1 - _c2)

        def range_reduce(dst_sl, src_sl, add0, after, tag, scr_sl):
            o1 = v(lambda e: e.tensor_scalar(dst_sl(), src_sl(), add0, INV2PI,
                                             ALU.add, ALU.mult),
                   after=after, name=f"{tag}.q")
            o2 = v(lambda e: e.tensor_scalar(scr_sl(), dst_sl(), MAGIC, MAGIC,
                                             ALU.add, ALU.subtract),
                   after=[o1], name=f"{tag}.k")
            o3 = v(lambda e: e.tensor_scalar(dst_sl(), src_sl(), add0, None,
                                             ALU.add),
                   after=[o2], name=f"{tag}.y")
            o4 = v(lambda e: e.scalar_tensor_tensor(dst_sl(), scr_sl(), -_c1,
                                                    dst_sl(), ALU.mult,
                                                    ALU.add),
                   after=[o2, o3], name=f"{tag}.c1")
            o5 = v(lambda e: e.scalar_tensor_tensor(dst_sl(), scr_sl(), -_c2,
                                                    dst_sl(), ALU.mult,
                                                    ALU.add),
                   after=[o4], name=f"{tag}.c2")
            o6 = v(lambda e: e.scalar_tensor_tensor(dst_sl(), scr_sl(), -_c3,
                                                    dst_sl(), ALU.mult,
                                                    ALU.add),
                   after=[o5], name=f"{tag}.c3")
            return o6

        c_e1 = v(lambda e: e.activation(colv[:, 2:3], colv[:, 0:1], AF.Exp),
                 after=[op_lnr_c], name="c_e1", eng="scalar")
        c_abs = v(lambda e: e.activation(colv[:, 3:4], colv[:, 2:3], AF.Exp,
                                         scale=negone_t[0:64, :]),
                  after=[c_e1, m_nego], name="c_abs", eng="scalar")
        c_th = range_reduce(lambda: colv[:, 4:5], lambda: colv[:, 1:2], 0.0,
                            [op_img_c], "c_th", lambda: colv[:, 8:9])
        c_th2 = range_reduce(lambda: colv[:, 5:6], lambda: colv[:, 1:2], HALF_PI,
                             [op_img_c, c_th], "c_th2", lambda: colv[:, 8:9])
        c_sin = v(lambda e: e.activation(colv[:, 6:7], colv[:, 4:5], AF.Sin),
                  after=[c_th], name="c_sin", eng="scalar")
        c_cos = v(lambda e: e.activation(colv[:, 7:8], colv[:, 5:6], AF.Sin),
                  after=[c_th2], name="c_cos", eng="scalar")
        c_p1r = v(lambda e: e.tensor_copy(bc[:, 4:5], colv[:, 7:8]),
                  after=[c_cos], name="p1r")
        c_p1i = v(lambda e: e.tensor_scalar(bc[:, 5:6], colv[:, 6:7], -1.0, None,
                                            ALU.mult),
                  after=[c_sin], name="p1i")
        m_p0r = ms("vector", lambda: Pr[:, 0:1], 1.0)
        m_p0i = ms("vector", lambda: Pi[:, 0:1], 0.0)
        lastP = [m_p0r, m_p0i, c_p1r, c_p1i]
        seg = 1
        curr, curi = c_p1r, c_p1i
        while seg < CH:
            w = min(seg, CH - seg)
            a1_ = v(lambda e, seg=seg, w=w: e.tensor_scalar(
                rotr[:, 0:w], Pr[:, 0:w], bc[:, 4:5], None, ALU.mult),
                after=lastP, name=f"P{seg}.a")
            a2_ = v(lambda e, seg=seg, w=w: e.tensor_scalar(
                roti[:, 0:w], Pi[:, 0:w], bc[:, 5:6], None, ALU.mult),
                after=lastP, name=f"P{seg}.b")
            a3_ = v(lambda e, seg=seg, w=w: e.tensor_scalar(
                scr0[0:64, 0:w], Pi[:, 0:w], bc[:, 4:5], None, ALU.mult),
                after=lastP, name=f"P{seg}.c")
            a4_ = v(lambda e, seg=seg, w=w: e.tensor_scalar(
                scr1[0:64, 0:w], Pr[:, 0:w], bc[:, 5:6], None, ALU.mult),
                after=lastP, name=f"P{seg}.d")
            a5_ = v(lambda e, seg=seg, w=w: e.tensor_tensor(
                Pr[:, seg:seg + w], rotr[:, 0:w], roti[:, 0:w], ALU.subtract),
                after=[a1_, a2_], name=f"P{seg}.e")
            a6_ = v(lambda e, seg=seg, w=w: e.tensor_tensor(
                Pi[:, seg:seg + w], scr0[0:64, 0:w], scr1[0:64, 0:w], ALU.add),
                after=[a3_, a4_], name=f"P{seg}.f")
            b1_ = v(lambda e: e.tensor_tensor(bc[:, 2:3], bc[:, 4:5], bc[:, 4:5],
                                              ALU.mult),
                    after=[a1_, a2_, a3_, a4_, curr, curi], name=f"P{seg}.g")
            b2_ = v(lambda e: e.tensor_tensor(bc[:, 3:4], bc[:, 5:6], bc[:, 5:6],
                                              ALU.mult),
                    after=[a1_, a2_, a3_, a4_, curr, curi], name=f"P{seg}.h")
            b3_ = v(lambda e: e.tensor_tensor(bc[:, 5:6], bc[:, 4:5], bc[:, 5:6],
                                              ALU.mult),
                    after=[b1_, b2_], name=f"P{seg}.i")
            b4_ = v(lambda e: e.tensor_scalar(bc[:, 5:6], bc[:, 5:6], 2.0, None,
                                              ALU.mult),
                    after=[b3_], name=f"P{seg}.j")
            b5_ = v(lambda e: e.tensor_tensor(bc[:, 4:5], bc[:, 2:3], bc[:, 3:4],
                                              ALU.subtract),
                    after=[b3_], name=f"P{seg}.k")
            curr, curi = b5_, b4_
            lastP = [a5_, a6_, b5_, b4_]
            seg *= 2
        p5a = v(lambda e: e.tensor_tensor(bc[:, 2:3], Pr[:, CH - 1:CH],
                                          colv[:, 7:8], ALU.mult),
                after=lastP + [c_cos], name="p512.a")
        p5b = v(lambda e: e.tensor_scalar(scr0[0:64, 0:1], Pi[:, CH - 1:CH],
                                          colv[:, 6:7], None, ALU.mult),
                after=lastP + [c_sin], name="p512.b")
        p5c = v(lambda e: e.tensor_tensor(bc[:, 2:3], bc[:, 2:3],
                                          scr0[0:64, 0:1], ALU.add),
                after=[p5a, p5b], name="p512.c")
        p5d = v(lambda e: e.tensor_scalar(scr0[0:64, 0:1], Pr[:, CH - 1:CH],
                                          colv[:, 6:7], None, ALU.mult),
                after=[p5c], name="p512.d")
        p5e = v(lambda e: e.tensor_scalar(scr1[0:64, 0:1], Pi[:, CH - 1:CH],
                                          colv[:, 7:8], None, ALU.mult),
                after=[p5c], name="p512.e")
        p5g = v(lambda e: e.tensor_tensor(bc[:, 3:4], scr1[0:64, 0:1],
                                          scr0[0:64, 0:1], ALU.subtract),
                after=[p5d, p5e], name="p512.g")
        op_p512 = [p5c, p5g]
        m_bcr = ms("vector", lambda: bc[:, 0:1], 1.0)
        m_bci = ms("vector", lambda: bc[:, 1:2], 0.0)
        m_absa = ms("vector", lambda: absa[:], 1.0)
        ab1 = v(lambda e: e.tensor_scalar(absa[:], absa[:],
                                          colv[:, 3:4], None, ALU.mult),
                after=[m_absa, c_abs], name="absa.s")
        op_absa = ab1

        def rv(i):
            return rowv[:, i * DS:(i + 1) * DS]

        r_e1 = v(lambda e: e.activation(rv(2), rv(0), AF.Exp),
                 after=[op_lnr_r], name="r_e1", eng="scalar")
        r_th = range_reduce(lambda: rv(4), lambda: rv(1), 0.0, [op_img_r],
                            "r_th", lambda: rv(13))
        r_th2 = range_reduce(lambda: rv(5), lambda: rv(1), HALF_PI,
                             [op_img_r, r_th], "r_th2", lambda: rv(13))
        r_sin = v(lambda e: e.activation(rv(6), rv(4), AF.Sin),
                  after=[r_th], name="r_sin", eng="scalar")
        r_cos = v(lambda e: e.activation(rv(7), rv(5), AF.Sin),
                  after=[r_th2], name="r_cos", eng="scalar")
        r_expn = v(lambda e: e.activation(rv(8), rv(2), AF.Exp,
                                          scale=negone_t[0:1, :]),
                   after=[r_e1, m_nego], name="r_expn", eng="scalar")
        r_n1 = v(lambda e: e.tensor_tensor(rv(9), rv(8), rv(7), ALU.mult),
                 after=[r_expn, r_cos], name="r_n1")
        r_n2 = v(lambda e: e.tensor_scalar(rv(9), rv(9), -1.0, None, ALU.add),
                 after=[r_n1], name="r_n2")
        r_n3 = v(lambda e: e.tensor_tensor(rv(13), rv(8), rv(6), ALU.mult),
                 after=[r_expn, r_sin], name="r_n3")
        r_d1 = v(lambda e: e.tensor_tensor(rv(10), rv(2), rv(2), ALU.mult),
                 after=[r_e1], name="r_d1")
        r_d2 = v(lambda e: e.tensor_tensor(rv(14), rv(1), rv(1), ALU.mult),
                 after=[op_img_r], name="r_d2")
        r_d3 = v(lambda e: e.tensor_tensor(rv(10), rv(10), rv(14), ALU.add),
                 after=[r_d1, r_d2], name="r_d3")
        r_d4 = v(lambda e: e.reciprocal(rv(10), rv(10)), after=[r_d3],
                 name="r_d4")
        r_c1 = v(lambda e: e.tensor_tensor(rv(11), rv(9), rv(2), ALU.mult),
                 after=[r_n2, r_e1], name="r_c1")
        r_c2 = v(lambda e: e.tensor_scalar(rv(11), rv(11), -1.0, None, ALU.mult),
                 after=[r_c1], name="r_c2")
        r_c3 = v(lambda e: e.tensor_tensor(rv(14), rv(13), rv(1), ALU.mult),
                 after=[r_n3], name="r_c3")
        r_c4 = v(lambda e: e.tensor_tensor(rv(11), rv(11), rv(14), ALU.add),
                 after=[r_c2, r_c3], name="r_c4")
        r_c5 = v(lambda e: e.tensor_tensor(rv(11), rv(11), rv(10), ALU.mult),
                 after=[r_c4, r_d4], name="r_c5")
        r_c6 = v(lambda e: e.tensor_tensor(rv(12), rv(13), rv(2), ALU.mult),
                 after=[r_n3, r_e1], name="r_c6")
        r_c7 = v(lambda e: e.tensor_scalar(rv(12), rv(12), -1.0, None, ALU.mult),
                 after=[r_c6], name="r_c7")
        r_c8 = v(lambda e: e.tensor_tensor(rv(14), rv(9), rv(1), ALU.mult),
                 after=[r_n2], name="r_c8")
        r_c9 = v(lambda e: e.tensor_tensor(rv(12), rv(12), rv(14), ALU.subtract),
                 after=[r_c7, r_c8], name="r_c9")
        r_c10 = v(lambda e: e.tensor_tensor(rv(12), rv(12), rv(10), ALU.mult),
                  after=[r_c9, r_d4], name="r_c10")
        bco = mm_group(banks[6], [
            (lambda: ones_f1[:], lambda: rowv[:, 11 * DS:13 * DS]),
        ], after=[r_c5, r_c10, m_onesf1], name="bcoef")
        bcc = v(lambda e: e.copy(crep[:], banks[6].t[:, 0:2 * DS]),
                after=[bco[-1]], name="bcoef.c", eng="scalar")
        banks[6].readers.append(bcc)
        bb_ops = []
        for dj in range(ND):
            o1 = v(lambda e, dj=dj: e.tensor_tensor(
                bbar[:, dj * 128:dj * 128 + DS],
                bts[:, dj * DS:(dj + 1) * DS], crep[:, 0:DS], ALU.mult),
                after=[bcc, ld_ops["bts"]], name=f"bbar{dj}.r")
            o2 = v(lambda e, dj=dj: e.tensor_tensor(
                bbar[:, dj * 128 + DS:(dj + 1) * 128],
                bts[:, dj * DS:(dj + 1) * DS], crep[:, DS:2 * DS], ALU.mult),
                after=[bcc, ld_ops["bts"]], name=f"bbar{dj}.i")
            bb_ops += [o1, o2]

        Xw = {}
        lnw = {}
        ln_readers = {}
        x_load = {}
        store_ops = []
        drep_w = [m_drep]
        drep_r = []
        carry_w = {b: [m_carry, m_carry2] for b in range(NB)}
        rot_w = []
        rot_r = []
        bc_w = [m_bcr, m_bci] + op_p512
        ufp_w = {}
        ufp_r = []
        acc_w = {}
        ssec_r = []
        dcol_r = []
        bus_r = []
        qrot_r = []
        gsc_r = []
        hsr_r = []
        hsrb_r = []
        W0w = {}
        W1w = {}
        W0r = []
        W1r = []
        xsq_w = {}
        xsq_r = {k: [] for k in range(NXQ)}
        mu_w = {0: list(bb_ops), 1: list(bb_ops)}
        scrw = {0: [p5c, p5g], 1: [p5g], 2: []}

        def emit_stats(tag, b, xb_ops):
            mmo1 = mm_group(banks[6], [
                (lambda: ones_bf[:], lambda b=b, dj=dj: LNh(b, dj))
                for dj in range(ND)],
                after=xb_ops + [m_onesbf], name=f"{tag}.st1")
            war7 = banks[7].take()
            sq_ops = []
            mmo2 = []
            for dj in range(ND):
                sq = v(lambda e, dj=dj, b=b: e.tensor_tensor(
                    xsq[:, (dj % NXQ) * R:(dj % NXQ + 1) * R], LNh(b, dj),
                    LNh(b, dj), ALU.mult),
                    after=[xb_ops[dj]] + xsq_r[dj % NXQ],
                    name=f"{tag}.sq{dj}")
                sq_ops.append(sq)
                a = [sq, m_onesbf] + (war7 if dj == 0 else [])
                piece = sch.add(
                    "tensor",
                    lambda e, dj=dj: e.matmul(
                        banks[7].t[:, 0:R], ones_bf[:],
                        xsq[:, (dj % NXQ) * R:(dj % NXQ + 1) * R],
                        start=(dj == 0), stop=(dj == ND - 1)),
                    after=a, name=f"{tag}.st2.{dj}")
                mmo2.append(piece)
                xsq_r[dj % NXQ] = [piece]
            mu_sl = mu[:, b * R:(b + 1) * R]
            rs_sl = rstd[:, b * R:(b + 1) * R]
            o_mu = v(lambda e, mu_sl=mu_sl: e.tensor_scalar(
                mu_sl, banks[6].t[:, 0:R], 1.0 / D, None, ALU.mult),
                after=[mmo1[-1]] + mu_w.get(b, []), name=f"{tag}.mu")
            banks[6].readers.append(o_mu)
            o_msq = v(lambda e, mu_sl=mu_sl: e.tensor_tensor(
                scr2[:], mu_sl, mu_sl, ALU.mult),
                after=[o_mu] + scrw[2], name=f"{tag}.msq")
            o_var = v(lambda e, rs_sl=rs_sl: e.scalar_tensor_tensor(
                rs_sl, banks[7].t[:, 0:R], 1.0 / D, scr2[:],
                ALU.mult, ALU.subtract),
                after=[mmo2[-1], o_msq], name=f"{tag}.var")
            banks[7].readers.append(o_var)
            scrw[2] = [o_msq, o_var]
            o_lnv = v(lambda e, rs_sl=rs_sl: e.activation(
                scr2[:], rs_sl, AF.Ln, bias=eps_t[:]),
                after=[o_var, m_eps] + scrw[2], name=f"{tag}.lnv",
                eng="scalar")
            o_rstd = v(lambda e, rs_sl=rs_sl: e.activation(
                rs_sl, scr2[:], AF.Exp, scale=negh_t[:]),
                after=[o_lnv, m_negh], name=f"{tag}.rstd", eng="scalar")
            scrw[2] = [o_lnv, o_rstd]
            mu_w[b] = [o_rstd]
            return mmo1, sq_ops, o_mu, o_rstd, mu_sl, rs_sl

        def emit_LN(c, b, want_ufp, tag, x_dep_extra=()):
            xb_ops = []
            war = ln_readers.get(b, [])
            for dj in range(ND):
                dep = [Xw.get((b, dj))] + list(x_dep_extra) + war
                if dj % 2 == 0:
                    o = v(lambda e, b=b, dj=dj: e.copy(LNh(b, dj), Xh(b, dj)),
                          after=dep, name=f"{tag}.xb{dj}", eng="scalar")
                else:
                    o = v(lambda e, b=b, dj=dj: e.tensor_copy(LNh(b, dj),
                                                              Xh(b, dj)),
                          after=dep, name=f"{tag}.xb{dj}", eng="gpsimd")
                xb_ops.append(o)
            ln_readers[b] = []
            mmo1, sq_ops, o_mu, o_rstd, mu_sl, rs_sl = emit_stats(
                tag, b, xb_ops)
            norm_ops = []
            for dj in range(ND):
                lane = dj % 2
                eng = "vector" if lane == 0 else "gpsimd"
                sc = scr0 if lane == 0 else scr1
                o_s = v(lambda e, b=b, dj=dj, mu_sl=mu_sl, sc=sc:
                        e.tensor_tensor(sc[:], Xh(b, dj), mu_sl,
                                        ALU.subtract),
                        after=[Xw.get((b, dj)), o_mu] + scrw[lane],
                        name=f"{tag}.n{dj}a", eng=eng)
                scrw[lane] = [o_s]
                if want_ufp:
                    o_u = v(lambda e, dj=dj, rs_sl=rs_sl, sc=sc:
                            e.tensor_tensor(
                                ufp[:, dj * R:(dj + 1) * R], sc[:], rs_sl,
                                ALU.mult),
                            after=[o_s, o_rstd] + ufp_r, name=f"{tag}.n{dj}u",
                            eng=eng)
                    o_n = v(lambda e, b=b, dj=dj: e.copy(
                        LNh(b, dj), ufp[:, dj * R:(dj + 1) * R]),
                        after=[o_u, xb_ops[dj]] + mmo1 + [sq_ops[dj]],
                        name=f"{tag}.n{dj}c", eng="scalar")
                    ufp_w[dj] = o_u
                    scrw[lane].append(o_u)
                    norm_ops.append(o_n)
                else:
                    o_n = v(lambda e, b=b, dj=dj, rs_sl=rs_sl, sc=sc:
                            e.tensor_tensor(LNh(b, dj), sc[:], rs_sl,
                                            ALU.mult),
                            after=[o_s, o_rstd, xb_ops[dj]] + mmo1 +
                            [sq_ops[dj]],
                            name=f"{tag}.n{dj}", eng=eng)
                    scrw[lane].append(o_n)
                    norm_ops.append(o_n)
            if want_ufp:
                ufp_r.clear()
            return norm_ops

        NRH = R

        Wbuf = [W0, W1]
        scrh = [scr0, scr1]

        def Wr(b):
            return W0r if b == 0 else W1r

        def load_x(c, b):
            dep = [st.get(("store", c - 1, b))] if c > 0 else []
            o = sch.dma("gpsimd",
                        lambda e, c=c, b=b: e.dma_start(
                            out=X[:, b * ND * R:(b + 1) * ND * R],
                            in_=x_hbm[c * NB + b]),
                        after=dep, name=f"x{c}.{b}")
            for dj in range(ND):
                Xw[(b, dj)] = o

        def store_x(c, b):
            dep = [Xw[(b, dj)] for dj in range(ND)]
            o_st = sch.dma("gpsimd",
                           lambda e, c=c, b=b: e.dma_start(
                               out=out_hbm[c * NB + b],
                               in_=X[:, b * ND * R:(b + 1) * ND * R]),
                           after=dep, name=f"store{c}.{b}")
            st[("store", c, b)] = o_st

        def s1A(c, b, norm):
            h_evac = {}
            for s in range(F // SM_A):
                slab = load_slab(wa02, s, 2 * ND * SM_A, f"a02.{c}.{b}.{s}")
                for fl in range(SM_A // 128):
                    ft = s * (SM_A // 128) + fl
                    bk0 = banks[(ft % 2) * 2]
                    bk1 = banks[(ft % 2) * 2 + 1]
                    mm0 = mm_group(bk0, [
                        (lambda slab=slab, kt=kt, fl=fl:
                         slab.t[:, kt * SM_A + fl * 128:
                                kt * SM_A + (fl + 1) * 128],
                         lambda b=b, kt=kt: LNh(b, kt))
                        for kt in range(ND)],
                        after=[slab.load_op] + norm, name=f"p0.{ft}.{b}")
                    mm1 = mm_group(bk1, [
                        (lambda slab=slab, kt=kt, fl=fl:
                         slab.t[:, ND * SM_A + kt * SM_A + fl * 128:
                                ND * SM_A + kt * SM_A + (fl + 1) * 128],
                         lambda b=b, kt=kt: LNh(b, kt))
                        for kt in range(ND)],
                        after=[slab.load_op] + norm, name=f"p2.{ft}.{b}")
                    slab.readers += [mm0[-1], mm1[-1]]
                    tscr = scrh[b]
                    o_t1 = v(lambda e, bk0=bk0, tscr=tscr: e.activation(
                        tscr[:], bk0.t[:, 0:NRH], AF.Silu,
                        scale=drep[:, 0:1]),
                        after=[mm0[-1]] + drep_w + scrw[b],
                        name=f"s1h.{ft}.{b}.a", eng="scalar")
                    drep_r.append(o_t1)
                    bk0.readers.append(o_t1)
                    o_h = v(lambda e, bk1=bk1, ft=ft, b=b, tscr=tscr:
                            e.tensor_tensor(
                                Wbuf[b][:, ft * NRH:(ft + 1) * NRH],
                                tscr[:], bk1.t[:, 0:NRH], ALU.mult),
                            after=[mm1[-1], o_t1] + Wr(b),
                            name=f"s1h.{ft}.{b}.b")
                    bk1.readers.append(o_h)
                    scrw[b] = [o_h]
                    h_evac[(b, ft)] = o_h
            if b == 0:
                W0r.clear()
            else:
                W1r.clear()
            return h_evac

        def s1B(c, b, norm, h_evac):
            loss_ops = []
            for s in range(D // SM_H):
                slab = load_slab(wa1, s, NF * SM_H, f"a1.{c}.{b}.{s}")
                for dl in range(SM_H // 128):
                    dj = s * (SM_H // 128) + dl
                    bk = banks[4 + (dj % 2)]
                    mmo = mm_group(bk, [
                        (lambda slab=slab, kt=kt, dl=dl:
                         slab.t[:, kt * SM_H + dl * 128:
                                kt * SM_H + (dl + 1) * 128],
                         lambda b=b, kt=kt:
                         Wbuf[b][:, kt * NRH:(kt + 1) * NRH])
                        for kt in range(NF)],
                        after=[slab.load_op, h_evac[(b, NF - 1)]],
                        name=f"aout.{dj}.{b}")
                    slab.readers.append(mmo[-1])
                    Wr(b).append(mmo[-1])
                    sA = scrh[b]
                    o_o = v(lambda e, bk=bk, sA=sA: e.tensor_scalar(
                        sA[:], bk.t[:, 0:NRH], drep[:, 1:2], None, ALU.mult),
                        after=[mmo[-1]] + drep_w + scrw[b],
                        name=f"al.{dj}.{b}.o")
                    drep_r.append(o_o)
                    bk.readers.append(o_o)
                    o_x = v(lambda e, b=b, dj=dj, sA=sA: e.tensor_tensor(
                        Xh(b, dj), Xh(b, dj), sA[:], ALU.add),
                        after=[o_o, Xw[(b, dj)]], name=f"al.{dj}.{b}.x")
                    Xw[(b, dj)] = o_x
                    o_d = v(lambda e, b=b, dj=dj, sA=sA: e.tensor_tensor(
                        sA[:], sA[:], LNh(b, dj), ALU.subtract),
                        after=[o_o, norm[dj]], name=f"al.{dj}.{b}.d")
                    ln_readers.setdefault(b, []).append(o_d)
                    o_q = v(lambda e, b=b, dj=dj, sA=sA:
                            e.scalar_tensor_tensor(
                                sA[:], sA[:], 1.0, sA[:], ALU.mult, ALU.mult,
                                accum_out=accs[:, b * ND + dj:
                                               b * ND + dj + 1]),
                            after=[o_d] + acc_w.get((b, dj), []),
                            name=f"al.{dj}.{b}.q")
                    scrw[b] = [o_d, o_q]
                    acc_w[(b, dj)] = [o_q]
                    loss_ops.append(o_q)
            st[("loss", c, b)] = loss_ops

        def sse_cc(c):
            lo = []
            for b in range(NB):
                lo += st[("loss", c, b)]
            o_r1 = v(lambda e: e.reduce_sum(ssec[:, 0:1], accs[:],
                                            axis=mybir.AxisListType.X),
                     after=lo + ssec_r, name=f"sse{c}.r")
            ssec_r.clear()
            for k in list(acc_w):
                acc_w[k] = [o_r1]
            mm_s = mm_group(banks[6], [
                (lambda: onesc[:], lambda: ssec[:, 0:1])],
                after=[o_r1, m_onesc], name=f"sse{c}.mm")
            ssec_r.append(mm_s[-1])
            o_sc = v(lambda e: e.copy(sser[:, 0:1], banks[6].t[0:1, 0:1]),
                     after=[mm_s[-1], m_sser] + st.get(("sserow_r", c), []),
                     name=f"sse{c}.c", eng="scalar")
            banks[6].readers.append(o_sc)
            o_wr = sch.dma("gpsimd",
                           lambda e, c=c: e.dma_start(out=cc_in[c].ap(),
                                                      in_=sser[:]),
                           after=[o_sc], name=f"sse{c}.wr")
            st[("sserow_r", c + 1)] = [o_wr]

            def cc_fn(e, c=c):
                return e.collective_compute(
                    "AllReduce", ALU.add,
                    replica_groups=[list(range(n_cores))],
                    ins=[cc_in[c].ap().opt()],
                    outs=[cc_out[c].ap().opt()])
            st[("cc", c)] = sch.add("gpsimd", cc_fn, after=[o_wr], inc=1,
                                    name=f"cc{c}", chan="cc")

        def d_update(c):
            o_cc = st[("cc", c)]
            o_rd = sch.dma("gpsimd",
                           lambda e, c=c: e.dma_start(out=sser[:],
                                                      in_=cc_out[c].ap()),
                           after=[o_cc, st[("sserow_r", c + 1)][0]],
                           name=f"ccrd{c}")
            o_dc = v(lambda e: e.tensor_scalar(
                dcol[:, 1:2], sser[0:1, 0:1], -LR / NTOT, 1.0,
                ALU.mult, ALU.add),
                after=[o_rd, m_dcol] + dcol_r, name=f"dec{c}")
            st[("sserow_r", c + 1)] = [o_dc]
            o_dm = v(lambda e: e.tensor_tensor(
                dcol[:, 0:1], dcol[:, 0:1], dcol[:, 1:2], ALU.mult),
                after=[o_dc], name=f"dmul{c}")
            o_d2 = v(lambda e: e.tensor_tensor(
                dcol[:, 2:3], dcol[:, 0:1], dcol[:, 0:1], ALU.mult),
                after=[o_dm], name=f"d2{c}")
            mm_d = mm_group(banks[7], [
                (lambda: ones_f1[:], lambda: dcol[:, 0:3])],
                after=[o_d2, m_onesf1], name=f"dbc{c}")
            dcol_r.clear()
            dcol_r.append(mm_d[-1])
            o_dr = v(lambda e: e.copy(drep[:], banks[7].t[:, 0:3:2]),
                     after=[mm_d[-1]] + drep_r, name=f"drep{c}", eng="scalar")
            banks[7].readers.append(o_dr)
            drep_w.clear()
            drep_w.append(o_dr)
            drep_r.clear()

        def ffn_w12(tagp, b, norm, preslabs=()):
            g_evac = {}
            for s in range(F // SM_A8):
                if s < len(preslabs):
                    slab = preslabs[s]
                else:
                    slab = load_slab(ww12, s, 2 * ND * SM_A8,
                                     f"{tagp}.w12.{b}.{s}", pool=1)
                for fl in range(SM_A8 // 128):
                    ft = s * (SM_A8 // 128) + fl
                    bk0 = banks[(ft % 2) * 2]
                    bk1 = banks[(ft % 2) * 2 + 1]
                    mm0 = mm_group(bk0, [
                        (lambda slab=slab, kt=kt, fl=fl:
                         slab.t[:, kt * SM_A8 + fl * 128:
                                kt * SM_A8 + (fl + 1) * 128],
                         lambda b=b, kt=kt: LNh(b, kt))
                        for kt in range(ND)],
                        after=[slab.load_op] + norm, name=f"{tagp}.w1.{ft}")
                    mm1 = mm_group(bk1, [
                        (lambda slab=slab, kt=kt, fl=fl:
                         slab.t[:, ND * SM_A8 + kt * SM_A8 + fl * 128:
                                ND * SM_A8 + kt * SM_A8 + (fl + 1) * 128],
                         lambda b=b, kt=kt: LNh(b, kt))
                        for kt in range(ND)],
                        after=[slab.load_op] + norm, name=f"{tagp}.w2.{ft}")
                    slab.readers += [mm0[-1], mm1[-1]]
                    tscr = scr0 if ft % 2 == 0 else scr1
                    o_t1 = v(lambda e, bk0=bk0, ft=ft, tscr=tscr:
                             e.activation(
                                 tscr[:], bk0.t[:, 0:NRH], AF.Silu,
                                 scale=s1s[:, ft:ft + 1]),
                             after=[mm0[-1], ld_ops["s1"]] + scrw[ft % 2],
                             name=f"{tagp}.g{ft}.a", eng="scalar")
                    bk0.readers.append(o_t1)
                    o_g = v(lambda e, bk1=bk1, ft=ft, tscr=tscr:
                            e.scalar_tensor_tensor(
                                W0[:, ft * NRH:(ft + 1) * NRH],
                                bk1.t[:, 0:NRH], s2s[:, ft:ft + 1],
                                tscr[:], ALU.mult, ALU.mult),
                            after=[mm1[-1], o_t1, ld_ops["s2"]] + W0r,
                            name=f"{tagp}.g{ft}.b")
                    bk1.readers.append(o_g)
                    scrw[ft % 2] = [o_g]
                    g_evac[ft] = o_g
            W0r.clear()
            return g_evac

        def ffn_h(tagp, phs, whbm, b, src_evac, srcW, dstW, dstWr, sc_sl,
                  bank_base, cb=None):
            out_evac = {}
            for s in range(F // SM_H8):
                slab = load_slab(whbm, s, NF * SM_H8,
                                 f"{tagp}.{phs}.{b}.{s}", pool=1)
                for ml in range(SM_H8 // 128):
                    mt = s * (SM_H8 // 128) + ml
                    if cb is not None and mt in cb:
                        cb.pop(mt)()
                    bk = banks[bank_base + (mt % 2)]
                    if mt == 0:
                        mmo = mm_group(bk, [
                            (lambda slab=slab, kt=kt, ml=ml:
                             slab.t[:, kt * SM_H8 + ml * 128:
                                    kt * SM_H8 + (ml + 1) * 128],
                             lambda kt=kt: srcW[:, kt * NRH:(kt + 1) * NRH])
                            for kt in range(NF)],
                            after=[slab.load_op],
                            per_after=[src_evac[kt] for kt in range(NF)],
                            name=f"{tagp}.{phs}.{mt}")
                    else:
                        mmo = mm_group(bk, [
                            (lambda slab=slab, kt=kt, ml=ml:
                             slab.t[:, kt * SM_H8 + ml * 128:
                                    kt * SM_H8 + (ml + 1) * 128],
                             lambda kt=kt: srcW[:, kt * NRH:(kt + 1) * NRH])
                            for kt in range(NF)],
                            after=[slab.load_op, src_evac[NF - 1]],
                            name=f"{tagp}.{phs}.{mt}")
                    slab.readers.append(mmo[-1])
                    (W0r if srcW is W0 else W1r).append(mmo[-1])
                    o_gx = v(lambda e, bk=bk, mt=mt: e.activation(
                        dstW[:, mt * NRH:(mt + 1) * NRH],
                        bk.t[:, 0:NRH], AF.Gelu, scale=sc_sl[:, mt:mt + 1]),
                        after=[mmo[-1], ld_ops["sh1" if phs == "h1"
                                               else "sh2"]] + dstWr,
                        name=f"{tagp}.g{phs}.{mt}", eng="scalar")
                    bk.readers.append(o_gx)
                    out_evac[mt] = o_gx
            dstWr.clear()
            return out_evac

        def ffn_w3(tagp, b, g2_evac):
            for s in range(D // SM_H8):
                slab = load_slab(ww3, s, NF * SM_H8,
                                 f"{tagp}.w3.{b}.{s}", pool=1)
                for dl in range(SM_H8 // 128):
                    dj = s * (SM_H8 // 128) + dl
                    bk = banks[2 + (dj % 2)]
                    mmo = mm_group(bk, [
                        (lambda slab=slab, kt=kt, dl=dl:
                         slab.t[:, kt * SM_H8 + dl * 128:
                                kt * SM_H8 + (dl + 1) * 128],
                         lambda kt=kt: W0[:, kt * NRH:(kt + 1) * NRH])
                        for kt in range(NF)],
                        after=[slab.load_op] +
                        ([] if dj == 0 else [g2_evac[NF - 1]]),
                        per_after=([g2_evac[kt] for kt in range(NF)]
                                   if dj == 0 else None),
                        name=f"{tagp}.w3.{dj}")
                    slab.readers.append(mmo[-1])
                    W0r.append(mmo[-1])
                    o_x = v(lambda e, bk=bk, b=b, dj=dj:
                            e.scalar_tensor_tensor(
                                Xh(b, dj), bk.t[:, 0:NRH],
                                s3s[:, dj:dj + 1], Xh(b, dj),
                                ALU.mult, ALU.add),
                            after=[mmo[-1], Xw[(b, dj)], ld_ops["s3"]],
                            name=f"{tagp}.xo.{dj}")
                    bk.readers.append(o_x)
                    Xw[(b, dj)] = o_x

        def s3_rot(c):
            nonlocal rot_w, bc_w
            rr1 = v(lambda e: e.tensor_scalar(rotr[:], Pr[:], bc[:, 0:1],
                                              None, ALU.mult),
                    after=lastP + bc_w + rot_r, name=f"rot{c}.a")
            rr2 = v(lambda e: e.tensor_scalar(scr0[0:64, 0:CH], Pi[:],
                                              bc[:, 1:2], None, ALU.mult),
                    after=lastP + bc_w + scrw[0], name=f"rot{c}.b")
            rr3 = v(lambda e: e.tensor_tensor(rotr[:], rotr[:],
                                              scr0[0:64, 0:CH],
                                              ALU.subtract),
                    after=[rr1, rr2], name=f"rot{c}.c")
            ri1 = v(lambda e: e.tensor_scalar(roti[:], Pi[:], bc[:, 0:1],
                                              None, ALU.mult),
                    after=lastP + bc_w + rot_r, name=f"rot{c}.d")
            ri2 = v(lambda e: e.tensor_scalar(scr0[0:64, 0:CH], Pr[:],
                                              bc[:, 1:2], None, ALU.mult),
                    after=[rr3] + bc_w, name=f"rot{c}.e")
            ri3 = v(lambda e: e.tensor_tensor(roti[:], roti[:],
                                              scr0[0:64, 0:CH], ALU.add),
                    after=[ri1, ri2], name=f"rot{c}.f")
            scrw[0] = [ri2, ri3]
            rot_w = [rr3, ri3]
            rot_r.clear()
            u1 = v(lambda e: e.tensor_tensor(colv[:, 9:10], bc[:, 0:1],
                                             bc[:, 2:3], ALU.mult),
                   after=[rr3, ri1] + bc_w, name=f"bc{c}.a")
            u2 = v(lambda e: e.tensor_tensor(colv[:, 10:11], bc[:, 1:2],
                                             bc[:, 3:4], ALU.mult),
                   after=[rr3, ri1] + bc_w, name=f"bc{c}.b")
            u3 = v(lambda e: e.tensor_tensor(colv[:, 11:12], bc[:, 0:1],
                                             bc[:, 3:4], ALU.mult),
                   after=[rr3, ri1] + bc_w, name=f"bc{c}.c")
            u4 = v(lambda e: e.tensor_tensor(bc[:, 1:2], bc[:, 1:2],
                                             bc[:, 2:3], ALU.mult),
                   after=[u1, u2, u3] + bc_w + [ri3, ri2], name=f"bc{c}.d")
            u5 = v(lambda e: e.tensor_tensor(bc[:, 0:1], colv[:, 9:10],
                                             colv[:, 10:11], ALU.subtract),
                   after=[u1, u2, u4], name=f"bc{c}.e")
            u6 = v(lambda e: e.tensor_tensor(bc[:, 1:2], colv[:, 11:12],
                                             bc[:, 1:2], ALU.add),
                   after=[u3, u4], name=f"bc{c}.f")
            bc_w = [u5, u6]

        def s3_scan(c, b, norm):
            mmb = mm_group(banks[4], [
                (lambda dj=dj: bbar[:, dj * 128:(dj + 1) * 128],
                 lambda b=b, dj=dj: LNh(b, dj))
                for dj in range(ND)],
                after=norm + bb_ops, name=f"bu{c}.{b}")
            ln_readers.setdefault(b, []).extend(mmb)
            o_bu = v(lambda e: e.copy(busR[:], banks[4].t[0:64, 0:CH]),
                     after=[mmb[-1]] + bus_r, name=f"bu{c}.{b}.r",
                     eng="scalar")
            o_bu2 = v(lambda e: e.copy(busI[:], banks[4].t[64:128, 0:CH]),
                      after=[mmb[-1]] + bus_r, name=f"bu{c}.{b}.i",
                      eng="scalar")
            bus_r.clear()
            banks[4].readers.extend([o_bu, o_bu2])
            q1 = v(lambda e: e.tensor_tensor(scr0[0:64, 0:CH], busR[:],
                                             rotr[:], ALU.mult),
                   after=[o_bu] + rot_w + scrw[0], name=f"q{c}{b}.1")
            q2 = v(lambda e: e.tensor_tensor(scr1[0:64, 0:CH], busI[:],
                                             roti[:], ALU.mult),
                   after=[o_bu2] + rot_w + scrw[1], name=f"q{c}{b}.2")
            q3 = v(lambda e: e.tensor_tensor(qR[:], scr0[0:64, 0:CH],
                                             scr1[0:64, 0:CH], ALU.subtract),
                   after=[q1, q2] + qrot_r, name=f"q{c}{b}.3")
            q4 = v(lambda e: e.tensor_tensor(scr0[0:64, 0:CH], busR[:],
                                             roti[:], ALU.mult),
                   after=[q3, o_bu] + rot_w, name=f"q{c}{b}.4")
            q5 = v(lambda e: e.tensor_tensor(scr1[0:64, 0:CH], busI[:],
                                             rotr[:], ALU.mult),
                   after=[q3, o_bu2] + rot_w, name=f"q{c}{b}.5")
            q6 = v(lambda e: e.tensor_tensor(qI[:], scr0[0:64, 0:CH],
                                             scr1[0:64, 0:CH], ALU.add),
                   after=[q4, q5] + qrot_r, name=f"q{c}{b}.6")
            qrot_r.clear()
            scrw[0] = [q4, q6]
            scrw[1] = [q5, q6]
            rot_r.extend([q1, q2, q4, q5])
            sc1 = v(lambda e, b=b: e.tensor_tensor_scan(
                gR[:], absa[:], qR[:], carryR[:, b:b + 1], ALU.mult, ALU.add),
                after=[q3, op_absa] + carry_w[b] + gsc_r,
                name=f"scan{c}{b}.r")
            sc2 = v(lambda e, b=b: e.tensor_tensor_scan(
                gI[:], absa[:], qI[:], carryI[:, b:b + 1], ALU.mult, ALU.add),
                after=[q6, op_absa] + carry_w[b] + gsc_r,
                name=f"scan{c}{b}.i")
            gsc_r.clear()
            qrot_r.extend([sc1, sc2])
            o_cw = v(lambda e, b=b: e.tensor_copy(
                carryR[:, b:b + 1], gR[:, CH - 1:CH]),
                after=[sc1, sc2], name=f"carry{c}{b}.r")
            o_cw2 = v(lambda e, b=b: e.tensor_copy(
                carryI[:, b:b + 1], gI[:, CH - 1:CH]),
                after=[sc1, sc2], name=f"carry{c}{b}.i")
            carry_w[b] = [o_cw, o_cw2]
            gsc_r.extend([o_cw, o_cw2])
            h1_ = v(lambda e: e.tensor_tensor(scr0[0:64, 0:CH], gR[:],
                                              rotr[:], ALU.mult),
                    after=[sc1] + rot_w + scrw[0], name=f"un{c}{b}.1")
            h2_ = v(lambda e: e.tensor_tensor(scr1[0:64, 0:CH], gI[:],
                                              roti[:], ALU.mult),
                    after=[sc2] + rot_w + scrw[1], name=f"un{c}{b}.2")
            h3_ = v(lambda e: e.tensor_tensor(hsr[:], scr0[0:64, 0:CH],
                                              scr1[0:64, 0:CH], ALU.add),
                    after=[h1_, h2_] + hsr_r, name=f"un{c}{b}.3")
            hsr_r.clear()
            gsc_r.extend([h1_, h2_])
            scrw[0] = [h1_, h3_]
            scrw[1] = [h2_, h3_]
            rot_r.extend([h1_, h2_])
            o_hb = v(lambda e: e.copy(hsrb[:], hsr[:]),
                     after=[h3_] + hsrb_r, name=f"un{c}{b}.c", eng="scalar")
            hsrb_r.clear()
            hsr_r.append(o_hb)
            bus_r.extend([q1, q2, q4, q5])
            return o_hb

        def s3_y(c, b, o_hb):
            t_ops = []
            for dj in range(ND):
                bk = banks[5 + (dj % 2)]
                mmo = mm_group(bk, [
                    (lambda dj=dj: cts[0:64, dj * 128:(dj + 1) * 128],
                     lambda: hsrb[:]),
                ], after=[o_hb, ld_ops["cts"]], name=f"y{c}{b}.{dj}")
                hsrb_r.append(mmo[-1])
                o_t = v(lambda e, bk=bk, dj=dj: e.scalar_tensor_tensor(
                    ufp[:, dj * R:(dj + 1) * R],
                    ufp[:, dj * R:(dj + 1) * R], dp1[:, dj:dj + 1],
                    bk.t[:, 0:NRH], ALU.mult, ALU.add),
                    after=[mmo[-1], ufp_w[dj], op_dp1], name=f"t{c}{b}.{dj}")
                bk.readers.append(o_t)
                ufp_w[dj] = o_t
                t_ops.append(o_t)
            return t_ops

        def s3_ln2(c, b, t_ops):
            war = ln_readers.get(b, [])
            xb2 = []
            for dj in range(ND):
                if dj % 2 == 0:
                    o = v(lambda e, b=b, dj=dj: e.copy(
                        LNh(b, dj), ufp[:, dj * R:(dj + 1) * R]),
                        after=[t_ops[dj]] + war, name=f"ln2x{dj}",
                        eng="scalar")
                else:
                    o = v(lambda e, b=b, dj=dj: e.tensor_copy(
                        LNh(b, dj), ufp[:, dj * R:(dj + 1) * R]),
                        after=[t_ops[dj]] + war, name=f"ln2x{dj}",
                        eng="gpsimd")
                xb2.append(o)
            ln_readers[b] = []
            mmo1, sq2, o_mu, o_rstd, mu_sl, rs_sl = emit_stats(
                f"c{c}b{b}.ln2", b, xb2)
            ln_readers[b] = list(mmo1) + list(sq2)
            for dj in range(ND):
                lane = dj % 2
                eng = "vector" if lane == 0 else "gpsimd"
                sc = scrh[lane]
                o_s = v(lambda e, dj=dj, mu_sl=mu_sl, sc=sc: e.tensor_tensor(
                    sc[:], ufp[:, dj * R:(dj + 1) * R], mu_sl, ALU.subtract),
                    after=[t_ops[dj], o_mu] + scrw[lane],
                    name=f"ln2n{dj}a", eng=eng)
                ufp_r.append(o_s)
                o_m = v(lambda e, rs_sl=rs_sl, sc=sc: e.tensor_tensor(
                    sc[:], sc[:], rs_sl, ALU.mult),
                    after=[o_s, o_rstd], name=f"ln2n{dj}b", eng=eng)
                o_g = v(lambda e, b=b, dj=dj, sc=sc: e.scalar_tensor_tensor(
                    sc[:], sc[:], gams[:, dj:dj + 1], Xh(b, dj),
                    ALU.mult, ALU.add),
                    after=[o_m, Xw[(b, dj)], ld_ops["gam"]],
                    name=f"ln2n{dj}c")
                o_xn = v(lambda e, b=b, dj=dj, sc=sc: e.tensor_scalar(
                    Xh(b, dj), sc[:], bets[:, dj:dj + 1], None, ALU.add),
                    after=[o_g, ld_ops["bet"]], name=f"ln2n{dj}d")
                scrw[lane] = [o_g, o_xn]
                Xw[(b, dj)] = o_xn

        for b in range(NB):
            load_x(0, b)
        norm1 = {b: emit_LN(0, b, False, f"c0b{b}.ln1") for b in range(NB)}
        for c in range(NC_CH):
            pre12 = [load_slab(ww12, s, 2 * ND * SM_A8,
                               f"c{c}s2.w12p.{s}", pool=1)
                     for s in range(min(2, F // SM_A8))]
            h0 = s1A(c, 0, norm1[0])
            if c > 0:
                norm1[1] = emit_LN(c, 1, False, f"c{c}b1.ln1")
            s1B(c, 0, norm1[0], h0)
            h1v = s1A(c, 1, norm1[1])
            norm2_0 = emit_LN(c, 0, False, f"c{c}s2b0.ln")
            s1B(c, 1, norm1[1], h1v)
            g = ffn_w12(f"c{c}s2", 0, norm2_0, pre12)
            if c < NC_CH - 1:
                sse_cc(c)
            norm2_1 = emit_LN(c, 1, False, f"c{c}s2b1.ln")
            g1 = ffn_h(f"c{c}s2", "h1", wh1, 0, g, W0, W1, W1r, sh1s, 4)
            g2 = ffn_h(f"c{c}s2", "h2", wh2, 0, g1, W1, W0, W0r, sh2s, 0)
            ffn_w3(f"c{c}s2", 0, g2)
            g = ffn_w12(f"c{c}s2", 1, norm2_1)
            norm3_0 = emit_LN(c, 0, True, f"c{c}b0.ln3")
            box = {}
            K1, K2 = max(1, NF // 4), max(2, 3 * NF // 4)
            cb = {K1: lambda: (s3_rot(c),
                               box.__setitem__('hb', s3_scan(c, 0, norm3_0))),
                  K2: lambda: box.__setitem__('t', s3_y(c, 0, box['hb']))}
            g1 = ffn_h(f"c{c}s2", "h1", wh1, 1, g, W0, W1, W1r, sh1s, 4, cb)
            cb = {K1: lambda: s3_ln2(c, 0, box['t']),
                  K2: lambda: box.__setitem__(
                      'n4', emit_LN(c, 0, False, f"c{c}s4b0.ln"))}
            g2 = ffn_h(f"c{c}s2", "h2", wh2, 1, g1, W1, W0, W0r, sh2s, 0, cb)
            norm4_0 = box['n4']
            ffn_w3(f"c{c}s2", 1, g2)
            g = ffn_w12(f"c{c}s4", 0, norm4_0)
            norm3_1 = emit_LN(c, 1, True, f"c{c}b1.ln3")
            cb = {K1: lambda: box.__setitem__('hb', s3_scan(c, 1, norm3_1)),
                  K2: lambda: box.__setitem__('t', s3_y(c, 1, box['hb']))}
            g1 = ffn_h(f"c{c}s4", "h1", wh1, 0, g, W0, W1, W1r, sh1s, 4, cb)
            cb = {K1: lambda: s3_ln2(c, 1, box['t']),
                  K2: lambda: box.__setitem__(
                      'n41', emit_LN(c, 1, False, f"c{c}s4b1.ln"))}
            g2 = ffn_h(f"c{c}s4", "h2", wh2, 0, g1, W1, W0, W0r, sh2s, 0, cb)
            norm4_1 = box['n41']
            ffn_w3(f"c{c}s4", 0, g2)
            store_x(c, 0)
            g = ffn_w12(f"c{c}s4", 1, norm4_1)
            if c < NC_CH - 1:
                load_x(c + 1, 0)
                d_update(c)
            cb = {}
            if c < NC_CH - 1:
                cb = {K2: lambda: norm1.__setitem__(
                    0, emit_LN(c + 1, 0, False, f"c{c + 1}b0.ln1"))}
            g1 = ffn_h(f"c{c}s4", "h1", wh1, 1, g, W0, W1, W1r, sh1s, 4, cb)
            g2 = ffn_h(f"c{c}s4", "h2", wh2, 1, g1, W1, W0, W0r, sh2s, 0)
            ffn_w3(f"c{c}s4", 1, g2)
            store_x(c, 1)
            if c < NC_CH - 1:
                load_x(c + 1, 1)

        sch.emit(nc, {"tensor": nc.tensor, "vector": nc.vector,
                      "scalar": nc.scalar, "sync": nc.sync,
                      "gpsimd": nc.gpsimd},
                 sems, csems, same_engine_edges=same_engine_edges)

    return nc



def pack_core_inputs(cfg, core, x_full, wd):
    cfg = cfg_derived(cfg)
    NB = cfg["NB"]
    xi = pack_x(np.ascontiguousarray(x_full[core * NB:(core + 1) * NB]), cfg)
    im = {"x_img": xi}
    im.update(wd)
    return im


def make_weight_images(cfg, a0, a1, a2, w1, w2, h1, h2, w3,
                       s1, s2, sh1, sh2, s3, lnr, imag, Bm, Cm, Dv, gam, bet):
    cfg = cfg_derived(cfg)
    bf = ml_dtypes.bfloat16
    f8 = mybir.dt.np(FP8)
    wd = {
        "wa02": pack_pair(a0.astype(bf), a2.astype(bf), 256),
        "wa1": pack_lhsT(a1.astype(bf), 128),
        "ww12": pack_pair(np.ascontiguousarray(w1.T).astype(f8),
                          np.ascontiguousarray(w2.T).astype(f8), 256),
        "wh1": pack_lhsT(np.ascontiguousarray(h1.T).astype(f8), 128),
        "wh2": pack_lhsT(np.ascontiguousarray(h2.T).astype(f8), 128),
        "ww3": pack_lhsT(np.ascontiguousarray(w3.T).astype(f8), 128),
        "s1s": pack_percol(s1.astype(np.float32)),
        "s2s": pack_percol(s2.astype(np.float32)),
        "sh1s": pack_percol(sh1.astype(np.float32)),
        "sh2s": pack_percol(sh2.astype(np.float32)),
        "s3s": pack_percol(s3.astype(np.float32)),
        "dvec": pack_percol(Dv.astype(np.float32)),
        "gam": pack_percol(gam.astype(np.float32)),
        "bet": pack_percol(bet.astype(np.float32)),
        "lnr": lnr.astype(np.float32),
        "imag": imag.astype(np.float32),
        "btmat": pack_btmat(Bm, cfg),
        "ctmat": np.ascontiguousarray(Cm.T).astype(bf),
    }
    return wd


def pack_btmat(Bm, cfg):
    DS, D = Bm.shape
    ND = D // 128
    bt = np.ascontiguousarray(Bm.T)
    v = bt.reshape(ND, 128, DS).transpose(1, 0, 2).reshape(128, ND * DS)
    return np.ascontiguousarray(v.astype(np.float32))


def kernel(L0_attn_a_x, L0_attn_a_w0, L0_attn_a_w1, L0_attn_a_w2,
           L0_attn_b_log_neg_real, L0_attn_b_imag, L0_attn_b_B_mat,
           L0_attn_b_C_mat, L0_attn_b_D, L0_attn_b_gamma, L0_attn_b_beta,
           L0_ffn_w1_w, L0_ffn_w1_s, L0_ffn_w2_w, L0_ffn_w2_s,
           L0_ffn_h1_w, L0_ffn_h1_s, L0_ffn_h2_w, L0_ffn_h2_s,
           L0_ffn_w3_w, L0_ffn_w3_s):
    cfg = cfg_derived(FULL_CFG)
    x = np.asarray(L0_attn_a_x, dtype=np.float32)
    wd = make_weight_images(
        cfg,
        np.asarray(L0_attn_a_w0, np.float32), np.asarray(L0_attn_a_w1, np.float32),
        np.asarray(L0_attn_a_w2, np.float32),
        np.asarray(L0_ffn_w1_w, np.float32), np.asarray(L0_ffn_w2_w, np.float32),
        np.asarray(L0_ffn_h1_w, np.float32), np.asarray(L0_ffn_h2_w, np.float32),
        np.asarray(L0_ffn_w3_w, np.float32),
        np.asarray(L0_ffn_w1_s, np.float32), np.asarray(L0_ffn_w2_s, np.float32),
        np.asarray(L0_ffn_h1_s, np.float32), np.asarray(L0_ffn_h2_s, np.float32),
        np.asarray(L0_ffn_w3_s, np.float32),
        np.asarray(L0_attn_b_log_neg_real, np.float32),
        np.asarray(L0_attn_b_imag, np.float32),
        np.asarray(L0_attn_b_B_mat, np.float32),
        np.asarray(L0_attn_b_C_mat, np.float32),
        np.asarray(L0_attn_b_D, np.float32),
        np.asarray(L0_attn_b_gamma, np.float32),
        np.asarray(L0_attn_b_beta, np.float32),
    )
    nc = build_program(FULL_CFG)
    in_maps = [pack_core_inputs(cfg, core, x, wd)
               for core in range(cfg["n_cores"])]
    res = run_bass_kernel_spmd(nc, in_maps, core_ids=list(range(cfg["n_cores"])))
    outs = []
    for core in range(cfg["n_cores"]):
        outs.append(unpack_out(res.results[core]["out_img"], cfg))
    return np.concatenate(outs, axis=0)
